# version 05
# baseline (speedup 1.0000x reference)
import os
import numpy as np
from contextlib import ExitStack

import concourse.bass as bass
import concourse.mybir as mybir
from concourse.bass_utils import run_bass_kernel_spmd

F32 = mybir.dt.float32
BF16 = mybir.dt.bfloat16
AF = mybir.ActivationFunctionType
ALU = mybir.AluOpType
AX = mybir.AxisListType

T = 4096
E = 1024
NPAIR = 8
HID = 4096
NEXP = 8
NT = T // 128
NTB = T // 512
NQ = 4
QTOK = T // NQ
NHG = 4
VS = 80

ENGINES = ("tensor", "scalar", "vector", "gpsimd", "sync")


class Builder:
    def __init__(self, nc):
        self.nc = nc
        self.ops = {e: [] for e in ENGINES}
        self.sems = {}
        self.counts = {}

    def make_sems(self, ctx, names):
        for n in names:
            self.sems[n] = ctx.enter_context(self.nc.semaphore(n))
            self.counts[n] = 0

    def emit(self, eng, fn, waits=(), inc=None, by=1):
        tok = None
        if inc is not None:
            self.counts[inc] += by
            tok = (inc, self.counts[inc])
        self.ops[eng].append((fn, [w for w in waits if w is not None], inc, by))
        return tok

    def snap(self, name):
        return (name, self.counts[name])

    def replay(self, eng, obj):
        seen = {}
        for fn, ws, inc, by in self.ops[eng]:
            need = {}
            for (s, v) in ws:
                if v > need.get(s, -1):
                    need[s] = v
            for s, v in need.items():
                if seen.get(s, 0) >= v:
                    continue
                obj.wait_ge(self.sems[s], v)
                seen[s] = v
            ins = fn(obj)
            if inc is not None:
                ins.then_inc(self.sems[inc], by)


def build_program(mode="full"):
    nc = bass.Bass("TRN2", target_bir_lowering=False)

    x_h = nc.dram_tensor("x", [T, E], F32, kind="ExternalInput").ap()
    wq_h = nc.dram_tensor("wq", [E, E], F32, kind="ExternalInput").ap()
    wk_h = nc.dram_tensor("wk", [E, E], F32, kind="ExternalInput").ap()
    wo_h = nc.dram_tensor("wo", [E, E], F32, kind="ExternalInput").ap()
    bo_h = nc.dram_tensor("bo", [128, E], F32, kind="ExternalInput").ap()
    rw_h = nc.dram_tensor("rw", [NEXP, E], F32, kind="ExternalInput").ap()
    w1_h = nc.dram_tensor("w1", [NEXP, E, HID], F32, kind="ExternalInput").ap()
    w2_h = nc.dram_tensor("w2", [NEXP, HID, E], F32, kind="ExternalInput").ap()
    id_h = nc.dram_tensor("ident_in", [128, 128], F32, kind="ExternalInput").ap()
    out_h = nc.dram_tensor("out", [T, E], F32, kind="ExternalOutput").ap()
    x1s_h = nc.dram_tensor("x1scratch", [T, E], F32, kind="Internal").ap()

    ctx = ExitStack()
    b = Builder(nc)
    em = b.emit
    b.make_sems(ctx, ["s_x", "s_w", "s_tr", "s_io", "s_st", "s_t", "s_sc", "s_v"])

    def sb(stack, name, shape, dt):
        return stack.enter_context(nc.sbuf_tensor(name, shape, dt))

    def psum(stack, name, shape, dt=F32):
        return stack.enter_context(nc.psum_tensor(name, shape, dt))

    xT = [sb(ctx, f"xT{cc}", [128, T], BF16) for cc in range(8)]
    bo_bc = sb(ctx, "bo_bc", [128, E], F32)
    comb_sb = sb(ctx, "comb_sb", [128, NT * 8], F32)
    epsc = sb(ctx, "epsc", [128, 1], F32)
    ident = sb(ctx, "ident", [128, 128], F32)

    scOT = ExitStack()
    OT = [sb(scOT, f"OT{dc}", [128, T], BF16) for dc in range(NPAIR)]

    scPair = ExitStack()
    QT = sb(scPair, "QT", [128, T], BF16)
    KT = sb(scPair, "KT", [128, T], BF16)
    Vp2 = [sb(scPair, f"Vp{i}", [128, 64 * VS], BF16) for i in range(2)]
    PT = [sb(scPair, f"PT{i}", [128, 2048], BF16) for i in range(2)]
    sqs = [sb(scPair, f"sqs{i}", [128, 512], BF16) for i in range(2)]
    Wqs = sb(scPair, "Wqs0", [128, 1024], BF16)
    Wks = sb(scPair, "Wks0", [128, 1024], BF16)
    xstage = [sb(scPair, f"xstage{i}", [128, E], F32) for i in range(2)]
    qparts = sb(scPair, "qparts", [128, 16], F32)
    normbuf = sb(scPair, "normbuf", [128, 8], F32)
    rinv_f = sb(scPair, "rinv_f", [1, 512], F32)
    rrow = [sb(scPair, f"rrow{i}", [1, 512], F32) for i in range(2)]
    rden_f = sb(scPair, "rden_f", [128, 512], F32)
    ones1x64 = sb(scPair, "ones1x64", [1, 128], F32)
    onesK = sb(scPair, "onesK", [128, 128], BF16)

    ps_su = psum(scPair, "ps_su", [128, 2048])
    ps_s = None
    ps_o = [psum(scPair, f"ps_o{i}", [128, 512]) for i in range(2)]
    ps_pr = [psum(scPair, f"ps_pr{i}", [128, 512]) for i in range(2)]

    t_bo = em("gpsimd", lambda g: g.dma_start(out=bo_bc[:, :], in_=bo_h[:, :]), inc="s_w", by=16)
    t_eps = em("vector", lambda v: v.memset(epsc[:, :], 1e-5), inc="s_v")
    t_o64 = em("vector", lambda v: v.memset(ones1x64[:, :], 1.0), inc="s_v")
    t_oK = em("vector", lambda v: v.memset(onesK[:, :], 1.0), inc="s_v")
    t_vp = [em("vector", lambda v, i=i: v.memset(Vp2[i][:, :], 1.0), inc="s_v") for i in range(2)]
    t_id = em("gpsimd", lambda g: g.dma_start(out=ident[:, :], in_=id_h[:, :]), inc="s_w", by=16)

    x_tr_tok = [None] * NT
    slot_free = [None, None]
    xstage_free = [None, None]
    for ti in range(NT):
        xb = ti % 2
        xl = em("sync", lambda s, ti=ti, xb=xb: s.dma_start(
            out=xstage[xb][:, :], in_=x_h[ti * 128:(ti + 1) * 128, :]),
            waits=[xstage_free[xb]], inc="s_io", by=16)
        for cc in range(8):
            sl = cc % 2
            col = 512 * sl
            trt = em("tensor", lambda t, cc=cc, col=col, xb=xb: t.transpose(
                ps_su[:, col:col + 128],
                xstage[xb][:, cc * 128:(cc + 1) * 128], ident[:, :]),
                waits=[xl, t_id, slot_free[sl]], inc="s_t")
            ev = em("vector", lambda v, ti=ti, cc=cc, col=col: v.tensor_copy(
                xT[cc][:, ti * 128:(ti + 1) * 128], ps_su[:, col:col + 128]),
                waits=[trt], inc="s_v")
            slot_free[sl] = ev
        x_tr_tok[ti] = slot_free[(8 - 1) % 2]
        xstage_free[xb] = b.snap("s_t")

    def xT_ready(tb):
        return x_tr_tok[tb * 4 + 3]

    wslice_tok = [None] * NPAIR
    proj_done_tok = [None] * NPAIR

    def emit_wslice(p):
        war = proj_done_tok[p - 1] if p >= 1 else None
        em("gpsimd", lambda g, p=p: g.dma_start(
            out=Wqs[:, :].rearrange("p (cc d) -> p cc d", d=128),
            in_=wq_h[:, p * 128:(p + 1) * 128].rearrange("(cc p) d -> p cc d", p=128)),
            waits=[war], inc="s_w", by=16)
        wslice_tok[p] = em("gpsimd", lambda g, p=p: g.dma_start(
            out=Wks[:, :].rearrange("p (cc d) -> p cc d", d=128),
            in_=wk_h[:, p * 128:(p + 1) * 128].rearrange("(cc p) d -> p cc d", p=128)),
            waits=[war], inc="s_w", by=16)

    emit_wslice(0)

    qt_free = None
    vp_free = [None, None]
    pt_free = [None, None]
    ps_s_free = [b.snap("s_v"), b.snap("s_v")]
    ps_o_free = [None, None]
    ps_pr_free = [None, None]
    sq_free = [None, None]
    rinv_free = [None]
    rrow_free = [None, None]
    last_exp_tok = None
    pending = [None]

    for p in range(NPAIR):
        pb = p % 2
        proj_last_mm = None
        vtr_tok = None
        for (Wsl, TGT, isK) in ((Wks, KT, True), (Wqs, QT, False)):
            for tb in range(NTB):
                pp = tb % 2
                mmtok = None
                for cc in range(8):
                    w = []
                    if cc == 0:
                        w = [wslice_tok[p], xT_ready(tb), ps_pr_free[pp]]
                        if tb == 0 and isK:
                            w.append(qt_free)
                    mmtok = em("tensor", lambda t, Wsl=Wsl, cc=cc, tb=tb, pp=pp: t.matmul(
                        ps_pr[pp][:, :], Wsl[:, cc * 128:(cc + 1) * 128],
                        xT[cc][:, tb * 512:(tb + 1) * 512],
                        start=(cc == 0), stop=(cc == 7)),
                        waits=w, inc="s_t" if cc == 7 else None)
                proj_last_mm = mmtok
                evtok = em("vector", lambda v, TGT=TGT, tb=tb, pp=pp: v.tensor_copy(
                    TGT[:, tb * 512:(tb + 1) * 512], ps_pr[pp][:, :]),
                    waits=[mmtok], inc="s_v")
                ps_pr_free[pp] = evtok
                if isK:
                    for h in range(2):
                        for kc in range(4 * tb, 4 * tb + 4):
                            w = [evtok]
                            if tb == 0 and h == 0 and kc == 0:
                                w += [vp_free[pb], t_vp[pb]]
                            vtr_tok = em("sync", lambda s, pb=pb, h=h, kc=kc: s.dma_start_transpose(
                                Vp2[pb][:, (h * 32 + kc) * VS:(h * 32 + kc) * VS + 64],
                                KT[h * 64:(h + 1) * 64, kc * 128:(kc + 1) * 128]),
                                waits=w, inc="s_tr", by=16)
        proj_evac_done = ps_pr_free[(NTB - 1) % 2]
        proj_done_tok[p] = proj_last_mm
        if p + 1 < NPAIR:
            emit_wslice(p + 1)

        for (SRC, col) in ((QT, 0), (KT, 8)):
            for tb in range(NTB):
                sp = tb % 2
                sqtok = em("vector", lambda v, SRC=SRC, tb=tb, sp=sp: v.tensor_tensor(
                    sqs[sp][:, :], SRC[:, tb * 512:(tb + 1) * 512],
                    SRC[:, tb * 512:(tb + 1) * 512], ALU.mult),
                    waits=[proj_evac_done, sq_free[sp]], inc="s_v")
                pp = tb % 2
                qntok = em("tensor", lambda t, sp=sp, pp=pp: t.matmul(
                    ps_pr[pp][:, :], onesK[:, :], sqs[sp][:, :],
                    start=True, stop=True),
                    waits=[sqtok, t_oK, ps_pr_free[pp]], inc="s_t")
                sq_free[sp] = qntok
                ps_pr_free[pp] = em("vector", lambda v, col=col, tb=tb, pp=pp: v.reduce_max(
                    qparts[:, col + tb:col + tb + 1], ps_pr[pp][:, :], axis=AX.X),
                    waits=[qntok], inc="s_v")
        tq = em("vector", lambda v: v.reduce_max(
            normbuf[:, 0:1], qparts[:, 0:8], axis=AX.X),
            waits=[ps_pr_free[0], ps_pr_free[1], last_exp_tok], inc="s_v")
        tk2 = em("vector", lambda v: v.reduce_max(
            normbuf[:, 1:2], qparts[:, 8:16], axis=AX.X), waits=[tq], inc="s_v")
        biastok = em("vector", lambda v: v.tensor_scalar(
            normbuf[:, 2:3], normbuf[:, 0:1], normbuf[:, 1:2],
            -1.0 / 64.0, ALU.add, ALU.mult), waits=[tk2], inc="s_v")

        def make_chain(p, h, qb, evtoks):
            hb = h * 64

            def doit():
                rtok = em("vector", lambda v, h=h: v.reciprocal(rinv_f[:, :], rrow[h][:, :]),
                          waits=[evtoks[1], rinv_free[0]], inc="s_v")
                bctok = em("tensor", lambda t: t.matmul(
                    ps_pr[1][:, :], ones1x64[:, :], rinv_f[:, :], start=True, stop=True),
                    waits=[rtok, t_o64, ps_pr_free[1]], inc="s_t")
                rinv_free[0] = bctok
                cptok = em("vector", lambda v: v.tensor_copy(rden_f[:, :], ps_pr[1][:, :]),
                           waits=[bctok], inc="s_v")
                nm = em("vector", lambda v, p=p, hb=hb, qb=qb: v.tensor_tensor(
                    OT[p][hb:hb + 64, qb * 512:(qb + 1) * 512],
                    OT[p][hb:hb + 64, qb * 512:(qb + 1) * 512],
                    rden_f[hb:hb + 64, :], ALU.mult),
                    waits=[cptok], inc="s_v")
                ps_pr_free[1] = cptok
                rrow_free[h] = rtok
            return doit

        def emit_S(qb, g):
            mm = None
            for sub in range(2):
                kc = g * 2 + sub
                for h in range(2):
                    hb = h * 64
                    w = []
                    if sub == 0 and h == 0:
                        w = [ps_s_free[0], biastok]
                    mm = em("tensor", lambda t, hb=hb, qb=qb, kc=kc, h=h, sub=sub: t.matmul(
                        ps_su[:, h * 1024 + sub * 512:h * 1024 + (sub + 1) * 512],
                        KT[hb:hb + 64, kc * 128:(kc + 1) * 128],
                        QT[hb:hb + 64, qb * 512:(qb + 1) * 512],
                        start=True, stop=True),
                        waits=w, inc="s_t" if (sub == 1 and h == 1) else None)
            return mm

        pv_last_h = [None, None]
        for qb in range(NTB):
            s_tok = emit_S(qb, 0)
            for g in range(16):
                if g == 4 and pending[0] is not None:
                    for c in pending[0]:
                        c()
                    pending[0] = None
                gb = g % 2
                exptok = em("scalar", lambda sc, gb=gb: sc.activation(
                    PT[gb][:, :], ps_su[:, :], AF.Exp,
                    bias=normbuf[:, 2:3], scale=1.0 / 32.0),
                    waits=[s_tok, pt_free[gb]], inc="s_sc")
                ps_s_free[0] = exptok
                last_exp_tok = exptok
                if g + 1 < 16:
                    s_tok = emit_S(qb, g + 1)
                for h in range(2):
                    pvt = None
                    for sub in range(2):
                        kc = g * 2 + sub
                        w = [exptok]
                        if g == 0 and sub == 0:
                            w += [vtr_tok, ps_o_free[h]]
                        pvt = em("tensor", lambda t, pb=pb, h=h, kc=kc, gb=gb, sub=sub, g=g: t.matmul(
                            ps_o[h][0:65, :],
                            Vp2[pb][:, (h * 32 + kc) * VS:(h * 32 + kc) * VS + 65],
                            PT[gb][:, h * 1024 + sub * 512:h * 1024 + (sub + 1) * 512],
                            start=(g == 0 and sub == 0), stop=(g == 15 and sub == 1)),
                            waits=w, inc="s_t" if sub == 1 else None)
                    pv_last_h[h] = pvt
                pt_free[gb] = pv_last_h[1]
            chains = []
            for h in range(2):
                hb = h * 64
                evo = em("vector", lambda v, p=p, hb=hb, qb=qb, h=h: v.tensor_copy(
                    OT[p][hb:hb + 64, qb * 512:(qb + 1) * 512], ps_o[h][0:64, :]),
                    waits=[pv_last_h[h]], inc="s_v")
                evr = em("vector", lambda v, h=h: v.tensor_copy(
                    rrow[h][:, :], ps_o[h][64:65, :]),
                    waits=[evo, rrow_free[h]], inc="s_v")
                ps_o_free[h] = evr
                chains.append(make_chain(p, h, qb, [evo, evr]))
            pending[0] = chains
        if pending[0] is not None:
            for c in pending[0]:
                c()
            pending[0] = None
        vp_free[pb] = pv_last_h[1]
        qt_free = pv_last_h[1]

    barrA = [b.snap("s_t"), b.snap("s_sc"), b.snap("s_v"), b.snap("s_tr"), b.snap("s_io")]

    scPair.close()

    NB = 3
    scOP = ExitStack()
    Wo_bf = sb(scOP, "Wo_bf", [128, 8192], BF16)
    xres = [sb(scOP, f"xres{i}", [128, E], F32) for i in range(NB)]
    ubuf = [sb(scOP, f"ubuf{i}", [128, E], F32) for i in range(NB)]
    usq_dump = sb(scOP, "usq_dump", [128, E], BF16)
    x1f = [sb(scOP, f"x1f{i}", [128, E], F32) for i in range(NB)]
    stats = sb(scOP, "stats", [128, 16 * NB], F32)
    ps_op = [psum(scOP, f"ps_op{i}", [128, 1024]) for i in range(NB)]
    ps_t1 = [psum(scOP, f"ps_t1{i}", [128, 128]) for i in range(2)]

    wo_tok = em("gpsimd", lambda g: g.dma_start(
        out=Wo_bf[:, :].rearrange("p (dc c) -> p dc c", c=1024),
        in_=wo_h[:, :].rearrange("(dc p) c -> p dc c", p=128)),
        waits=barrA, inc="s_w", by=16)

    x1_tr_tok = [None] * NT
    trq = []
    ps_op_free = [None] * NB
    ubuf_free = [None] * NB
    x1f_free = [[], [], []]
    t1_free = [None, None]
    st_total = 0

    for ti in range(NT):
        tp = ti % NB
        so = 16 * tp
        xl = em("sync", lambda s, ti=ti, tp=tp: s.dma_start(
            out=xres[tp][:, :], in_=x_h[ti * 128:(ti + 1) * 128, :]),
            waits=barrA + [ubuf_free[tp]], inc="s_io", by=16)
        mmtok = None
        for cc2 in range(2):
            for dc in range(8):
                w = []
                if cc2 == 0 and dc == 0:
                    w = barrA + [wo_tok, ps_op_free[tp]]
                mmtok = em("tensor", lambda t, ti=ti, tp=tp, cc2=cc2, dc=dc: t.matmul(
                    ps_op[tp][:, cc2 * 512:(cc2 + 1) * 512],
                    OT[dc][:, ti * 128:(ti + 1) * 128],
                    Wo_bf[:, dc * 1024 + cc2 * 512:dc * 1024 + (cc2 + 1) * 512],
                    start=(dc == 0), stop=(dc == 7)),
                    waits=w, inc="s_t" if (dc == 7 and cc2 == 1) else None)
        ut1 = em("vector", lambda v, tp=tp: v.tensor_tensor(
            ubuf[tp][:, :], xres[tp][:, :], bo_bc[:, :], ALU.add),
            waits=[xl, t_bo] + x1f_free[tp], inc="s_v")
        ut2 = em("vector", lambda v, tp=tp: v.tensor_tensor(
            ubuf[tp][:, :], ubuf[tp][:, :], ps_op[tp][:, :], ALU.add),
            waits=[ut1, mmtok], inc="s_v")
        ps_op_free[tp] = ut2
        sstok = em("scalar", lambda sc, tp=tp, so=so: sc.activation(
            usq_dump[:, :], ubuf[tp][:, :], AF.Square, accum_out=stats[:, so:so + 1]),
            waits=[ut2], inc="s_sc")
        s1tok = em("scalar", lambda sc, tp=tp, so=so: sc.activation(
            usq_dump[:, :], ubuf[tp][:, :], AF.Copy, accum_out=stats[:, so + 1:so + 2]),
            waits=[sstok], inc="s_sc")
        smtok = em("vector", lambda v, so=so: v.tensor_scalar(
            stats[:, so + 5:so + 6], stats[:, so + 1:so + 2], 1.0 / E, None, ALU.mult),
            waits=[s1tok], inc="s_v")
        sm2tok = em("vector", lambda v, so=so: v.tensor_scalar(
            stats[:, so + 6:so + 7], stats[:, so + 5:so + 6], stats[:, so + 5:so + 6],
            None, ALU.mult), waits=[smtok], inc="s_v")
        vartok = em("vector", lambda v, so=so: v.tensor_scalar(
            stats[:, so + 7:so + 8], stats[:, so:so + 1], 1.0 / E,
            stats[:, so + 6:so + 7], ALU.mult, ALU.subtract),
            waits=[sm2tok], inc="s_v")
        lntok = em("scalar", lambda sc, so=so: sc.activation(
            stats[:, so + 8:so + 9], stats[:, so + 7:so + 8], AF.Ln, bias=epsc[:, 0:1]),
            waits=[vartok, t_eps], inc="s_sc")
        rstok = em("scalar", lambda sc, so=so: sc.activation(
            stats[:, so + 9:so + 10], stats[:, so + 8:so + 9], AF.Exp, scale=-0.5),
            waits=[lntok], inc="s_sc")
        x1tok = em("vector", lambda v, tp=tp, so=so: v.tensor_scalar(
            x1f[tp][:, :], ubuf[tp][:, :], stats[:, so + 5:so + 6],
            stats[:, so + 9:so + 10], ALU.subtract, ALU.mult),
            waits=[rstok], inc="s_v")
        ubuf_free[tp] = x1tok
        x1st = em("sync", lambda s, ti=ti, tp=tp: s.dma_start(
            out=x1s_h[ti * 128:(ti + 1) * 128, :], in_=x1f[tp][:, :]),
            waits=[x1tok], inc="s_io", by=16)
        frees = [x1st]
        if mode == "ln1":
            frees.append(em("sync", lambda s, ti=ti, tp=tp: s.dma_start(
                out=out_h[ti * 128:(ti + 1) * 128, :], in_=x1f[tp][:, :]),
                waits=[x1tok], inc="s_st", by=16))
            st_total += 16
        def emit_x1tr(ti, tp, x1tok, frees):
            for cc in range(8):
                sl = cc % 2
                trt = em("tensor", lambda t, tp=tp, cc=cc, sl=sl: t.transpose(
                    ps_t1[sl][:, :], x1f[tp][:, cc * 128:(cc + 1) * 128], ident[:, :]),
                    waits=[x1tok, t1_free[sl]], inc="s_t")
                t1_free[sl] = em("vector", lambda v, ti=ti, cc=cc, sl=sl: v.tensor_copy(
                    xT[cc][:, ti * 128:(ti + 1) * 128], ps_t1[sl][:, :]),
                    waits=[trt], inc="s_v")
            x1_tr_tok[ti] = t1_free[(8 - 1) % 2]
            frees.append(b.snap("s_t"))
            x1f_free[tp] = frees
        trq.append((ti, tp, x1tok, frees))
        if len(trq) > 2:
            emit_x1tr(*trq.pop(0))
        if ti == NT - 1:
            while trq:
                emit_x1tr(*trq.pop(0))

    if mode == "ln1":
        em("vector", lambda v: v.memset(stats[:, 15:16], 0.0),
           waits=[("s_st", st_total)], inc="s_v")
        blk = ctx.enter_context(nc.Block())
        for eng in ENGINES:
            getattr(blk, eng)(lambda obj, e=eng: b.replay(e, obj))
        scOP.close()
        scOT.close()
        ctx.close()
        return nc

    barrOP = [b.snap("s_t"), b.snap("s_sc"), b.snap("s_v"), b.snap("s_io")]
    scOP.close()

    scR = ExitStack()
    rw_bf = sb(scR, "rw_bf", [16, E], BF16)
    rwT = sb(scR, "rwT", [128, 128], BF16)
    lg = sb(scR, "lg", [128, 256], F32)
    Ee = sb(scR, "Ee", [128, 256], F32)
    msk = sb(scR, "msk", [128, 256], F32)
    gee = sb(scR, "gee", [128, 256], F32)
    m1b = sb(scR, "m1b", [128, 32], F32)
    m2b = sb(scR, "m2b", [128, 32], F32)
    s12 = sb(scR, "s12", [128, 32], F32)
    ri32 = sb(scR, "ri32", [128, 32], F32)
    m20c = sb(scR, "m20c", [128, 1], F32)
    ps_r = [psum(scR, f"ps_r{i}", [128, 8]) for i in range(2)]

    def g3(ap):
        return ap.rearrange("p (t e) -> p t e", e=8)

    def b3(ap):
        return ap.rearrange("p (t o) -> p t o", o=1).broadcast_to([128, 32, 8])

    rw_ld = em("gpsimd", lambda g: g.dma_start(out=rw_bf[0:8, :], in_=rw_h[:, :]),
               waits=barrOP, inc="s_w", by=16)
    t_m20 = em("vector", lambda v: v.memset(m20c[:, :], -20.0), inc="s_v")
    rwt_tok = None
    for cc in range(8):
        rwt_tok = em("sync", lambda s, cc=cc: s.dma_start_transpose(
            rwT[:, cc * 16:(cc + 1) * 16], rw_bf[:, cc * 128:(cc + 1) * 128]),
            waits=[rw_ld], inc="s_tr", by=16)

    ps_r_free = [None, None]
    ev_last = None
    for ti in range(NT):
        rb = ti % 2
        mmtok = None
        for cc in range(8):
            w = []
            if cc == 0:
                w = [rwt_tok, x1_tr_tok[ti], ps_r_free[rb]] + (barrOP if ti == 0 else [])
            mmtok = em("tensor", lambda t, ti=ti, cc=cc, rb=rb: t.matmul(
                ps_r[rb][:, :], xT[cc][:, ti * 128:(ti + 1) * 128], rwT[:, cc * 16:cc * 16 + 8],
                start=(cc == 0), stop=(cc == 7)), waits=w, inc="s_t" if cc == 7 else None)
        ev_last = em("vector", lambda v, ti=ti, rb=rb: v.tensor_copy(
            lg[:, ti * 8:(ti + 1) * 8], ps_r[rb][:, :]), waits=[mmtok], inc="s_v")
        ps_r_free[rb] = ev_last
    ex = em("scalar", lambda sc: sc.activation(Ee[:, :], lg[:, :], AF.Exp, bias=m20c[:, 0:1]),
            waits=[ev_last, t_m20], inc="s_sc")
    c1 = em("vector", lambda v: v.reduce_max(m1b[:, :], g3(Ee[:, :]), axis=AX.X),
            waits=[ex], inc="s_v")
    c2 = em("vector", lambda v: v.tensor_tensor(g3(msk[:, :]), g3(Ee[:, :]), b3(m1b[:, :]),
                                                ALU.is_ge), waits=[c1], inc="s_v")
    c3 = em("vector", lambda v: v.scalar_tensor_tensor(msk[:, :], msk[:, :], -1e30,
                                                       Ee[:, :], ALU.mult, ALU.add),
            waits=[c2], inc="s_v")
    c4 = em("vector", lambda v: v.reduce_max(m2b[:, :], g3(msk[:, :]), axis=AX.X),
            waits=[c3], inc="s_v")
    c5 = em("vector", lambda v: v.tensor_tensor(g3(gee[:, :]), g3(Ee[:, :]), b3(m2b[:, :]),
                                                ALU.is_ge), waits=[c4], inc="s_v")
    c6 = em("vector", lambda v: v.tensor_tensor(s12[:, :], m1b[:, :], m2b[:, :], ALU.add),
            waits=[c4], inc="s_v")
    c7 = em("vector", lambda v: v.reciprocal(ri32[:, :], s12[:, :]), waits=[c6], inc="s_v")
    c8 = em("vector", lambda v: v.tensor_tensor(gee[:, :], gee[:, :], Ee[:, :], ALU.mult),
            waits=[c5], inc="s_v")
    em("vector", lambda v: v.tensor_tensor(g3(comb_sb[:, :]), g3(gee[:, :]), b3(ri32[:, :]),
                                           ALU.mult), waits=[c7, c8], inc="s_v")
    comb_done = b.snap("s_v")

    barrB = [b.snap("s_t"), b.snap("s_sc"), b.snap("s_v"), b.snap("s_tr"), b.snap("s_io")]
    scR.close()
    scOT.close()

    scF = ExitStack()
    acc = sb(scF, "acc", [128, 8 * E], F32)
    w1b = [sb(scF, f"w1b{i}", [128, 8192], BF16) for i in range(2)]
    w2b = [sb(scF, f"w2b{i}", [128, 8192], BF16) for i in range(2)]
    hT = [sb(scF, f"hT{i}", [128, 512], BF16) for i in range(8)]
    x1res = [sb(scF, f"x1res{i}", [128, E], F32) for i in range(2)]
    u2buf = [sb(scF, f"u2buf{i}", [128, E], F32) for i in range(2)]
    o2buf = [sb(scF, f"o2buf{i}", [128, E], F32) for i in range(2)]
    usq2 = sb(scF, "usq2", [128, E], BF16)
    stats2 = sb(scF, "stats2", [128, 32], F32)
    ps_h = [psum(scF, f"ps_h{i}", [128, 512]) for i in range(2)]
    ps_y = [psum(scF, f"ps_y{i}", [128, 512]) for i in range(2)]

    NSLICE = NQ * NEXP * NHG
    wready = [None] * NSLICE
    wfree = [None] * NSLICE

    def emit_wdma(s):
        q, rem = divmod(s, NEXP * NHG)
        e, hg = divmod(rem, NHG)
        buf = s % 2
        war = wfree[s - 2] if s >= 2 else None
        w = [war] + (barrB if s < 2 else [])
        em("gpsimd", lambda g, e=e, hg=hg, buf=buf: g.dma_start(
            out=w1b[buf][:, :].rearrange("p (cc h) -> p cc h", h=1024),
            in_=w1_h[e, :, hg * 1024:(hg + 1) * 1024].rearrange("(cc p) h -> p cc h", p=128)),
            waits=w, inc="s_w", by=16)
        wready[s] = em("gpsimd", lambda g, e=e, hg=hg, buf=buf: g.dma_start(
            out=w2b[buf][:, :].rearrange("p (hc c) -> p hc c", c=1024),
            in_=w2_h[e, hg * 1024:(hg + 1) * 1024, :].rearrange("(hc p) c -> p hc c", p=128)),
            waits=w, inc="s_w", by=16)

    emit_wdma(0)
    emit_wdma(1)

    ps_h_free = [None, None]
    ps_y_free = [None, None]
    st_total = 0
    x1res_free = [None, None]
    u2_free = [None, None]
    o2_free = [None, None]

    for q in range(NQ):
        for e in range(NEXP):
            for hg in range(NHG):
                s = (q * NEXP + e) * NHG + hg
                buf = s % 2
                wlast = None
                for tb in range(2):
                    tok0 = q * QTOK + tb * 512
                    gelu_toks = []
                    for hc in range(8):
                        hb2 = hc % 2
                        mmtok = None
                        for cc in range(8):
                            w = []
                            if cc == 0:
                                w = [wready[s], ps_h_free[hb2]]
                                if e == 0 and hg == 0 and tb == 0 and hc == 0:
                                    w = w + [x1_tr_tok[q * 8 + 7]]
                                if s < 2 and tb == 0 and hc == 0:
                                    w = w + barrB
                            mmtok = em("tensor", lambda t, buf=buf, cc=cc, hc=hc, tok0=tok0, hb2=hb2: t.matmul(
                                ps_h[hb2][:, :],
                                w1b[buf][:, cc * 1024 + hc * 128:cc * 1024 + (hc + 1) * 128],
                                xT[cc][:, tok0:tok0 + 512],
                                start=(cc == 0), stop=(cc == 7)),
                                waits=w, inc="s_t" if cc == 7 else None)
                        gt = em("scalar", lambda sc, hc=hc, hb2=hb2: sc.activation(
                            hT[hc][:, :], ps_h[hb2][:, :], AF.Gelu), waits=[mmtok], inc="s_sc")
                        ps_h_free[hb2] = gt
                        gelu_toks.append(gt)
                    for tt2 in range(4):
                        ti_q = tb * 4 + tt2
                        tg = q * 8 + ti_q
                        for cc2 in range(2):
                            yb = (tt2 * 2 + cc2) % 2
                            mmtok = None
                            for hc in range(8):
                                w = []
                                if hc == 0:
                                    w = [gelu_toks[7], ps_y_free[yb]]
                                mmtok = em("tensor", lambda t, buf=buf, hc=hc, tt2=tt2, cc2=cc2, yb=yb: t.matmul(
                                    ps_y[yb][:, :],
                                    hT[hc][:, tt2 * 128:(tt2 + 1) * 128],
                                    w2b[buf][:, hc * 1024 + cc2 * 512:hc * 1024 + (cc2 + 1) * 512],
                                    start=(hc == 0), stop=(hc == 7)),
                                    waits=w, inc="s_t" if hc == 7 else None)
                            wlast = mmtok
                            if e == 0 and hg == 0:
                                ev = em("vector", lambda v, ti_q=ti_q, cc2=cc2, yb=yb, tg=tg, e=e: v.tensor_scalar(
                                    acc[:, ti_q * 1024 + cc2 * 512:ti_q * 1024 + (cc2 + 1) * 512],
                                    ps_y[yb][:, :], comb_sb[:, tg * 8 + e:tg * 8 + e + 1],
                                    None, ALU.mult),
                                    waits=[mmtok, comb_done if s == 0 else None], inc="s_v")
                            else:
                                ev = em("vector", lambda v, ti_q=ti_q, cc2=cc2, yb=yb, tg=tg, e=e: v.scalar_tensor_tensor(
                                    acc[:, ti_q * 1024 + cc2 * 512:ti_q * 1024 + (cc2 + 1) * 512],
                                    ps_y[yb][:, :], comb_sb[:, tg * 8 + e:tg * 8 + e + 1],
                                    acc[:, ti_q * 1024 + cc2 * 512:ti_q * 1024 + (cc2 + 1) * 512],
                                    ALU.mult, ALU.add), waits=[mmtok], inc="s_v")
                            ps_y_free[yb] = ev
                wfree[s] = wlast
                if s + 2 < NSLICE:
                    emit_wdma(s + 2)
        for ti_q in range(8):
            ti = q * 8 + ti_q
            tp = ti_q % 2
            so = 16 * tp
            xl = em("sync", lambda sy, ti=ti, tp=tp: sy.dma_start(
                out=x1res[tp][:, :], in_=x1s_h[ti * 128:(ti + 1) * 128, :]),
                waits=x1f_free[0] + x1f_free[1] + x1f_free[2] + [x1res_free[tp]],
                inc="s_io", by=16)
            ut = em("vector", lambda v, ti_q=ti_q, tp=tp: v.tensor_tensor(
                u2buf[tp][:, :], acc[:, ti_q * 1024:(ti_q + 1) * 1024], x1res[tp][:, :], ALU.add),
                waits=[xl, ps_y_free[0], ps_y_free[1], u2_free[tp]], inc="s_v")
            x1res_free[tp] = ut
            sstok = em("scalar", lambda sc, tp=tp, so=so: sc.activation(
                usq2[:, :], u2buf[tp][:, :], AF.Square, accum_out=stats2[:, so:so + 1]),
                waits=[ut], inc="s_sc")
            s1tok = em("scalar", lambda sc, tp=tp, so=so: sc.activation(
                usq2[:, :], u2buf[tp][:, :], AF.Copy, accum_out=stats2[:, so + 1:so + 2]),
                waits=[sstok], inc="s_sc")
            smtok = em("vector", lambda v, so=so: v.tensor_scalar(
                stats2[:, so + 5:so + 6], stats2[:, so + 1:so + 2], 1.0 / E, None,
                ALU.mult), waits=[s1tok], inc="s_v")
            sm2tok = em("vector", lambda v, so=so: v.tensor_scalar(
                stats2[:, so + 6:so + 7], stats2[:, so + 5:so + 6],
                stats2[:, so + 5:so + 6], None, ALU.mult), waits=[smtok], inc="s_v")
            vartok = em("vector", lambda v, so=so: v.tensor_scalar(
                stats2[:, so + 7:so + 8], stats2[:, so:so + 1],
                1.0 / E, stats2[:, so + 6:so + 7], ALU.mult, ALU.subtract),
                waits=[sm2tok], inc="s_v")
            lntok = em("scalar", lambda sc, so=so: sc.activation(
                stats2[:, so + 8:so + 9], stats2[:, so + 7:so + 8], AF.Ln, bias=epsc[:, 0:1]),
                waits=[vartok, t_eps], inc="s_sc")
            rstok = em("scalar", lambda sc, so=so: sc.activation(
                stats2[:, so + 9:so + 10], stats2[:, so + 8:so + 9], AF.Exp, scale=-0.5),
                waits=[lntok], inc="s_sc")
            otok = em("vector", lambda v, tp=tp, so=so: v.tensor_scalar(
                o2buf[tp][:, :], u2buf[tp][:, :], stats2[:, so + 5:so + 6],
                stats2[:, so + 9:so + 10], ALU.subtract, ALU.mult),
                waits=[rstok, o2_free[tp]], inc="s_v")
            u2_free[tp] = otok
            sttok = em("sync", lambda sy, ti=ti, tp=tp: sy.dma_start(
                out=out_h[ti * 128:(ti + 1) * 128, :], in_=o2buf[tp][:, :]),
                waits=[otok], inc="s_st", by=16)
            o2_free[tp] = sttok
            st_total += 16

    em("vector", lambda v: v.memset(stats2[:, 15:16], 0.0),
       waits=[("s_st", st_total)], inc="s_v")

    blk = ctx.enter_context(nc.Block())
    for eng in ENGINES:
        getattr(blk, eng)(lambda obj, e=eng: b.replay(e, obj))
    scF.close()
    ctx.close()
    return nc


_prog_cache = {}


def _get_prog(mode):
    if mode not in _prog_cache:
        _prog_cache[mode] = build_program(mode)
    return _prog_cache[mode]


def kernel(L0_attn_x, L0_attn_Wk, L0_attn_Wq, L0_attn_Wo, L0_attn_bo,
           L0_ffn_router_w, L0_ffn_w1, L0_ffn_w2, mode="full"):
    x = np.ascontiguousarray(np.asarray(L0_attn_x, dtype=np.float32))
    wk = np.ascontiguousarray(np.asarray(L0_attn_Wk, dtype=np.float32))
    wq = np.ascontiguousarray(np.asarray(L0_attn_Wq, dtype=np.float32))
    wo = np.ascontiguousarray(np.asarray(L0_attn_Wo, dtype=np.float32))
    bo = np.ascontiguousarray(np.broadcast_to(
        np.asarray(L0_attn_bo, dtype=np.float32).reshape(1, E), (128, E)))
    rw = np.ascontiguousarray(np.asarray(L0_ffn_router_w, dtype=np.float32))
    w1 = np.ascontiguousarray(np.asarray(L0_ffn_w1, dtype=np.float32))
    w2 = np.ascontiguousarray(np.asarray(L0_ffn_w2, dtype=np.float32))

    nc = _get_prog(mode)
    ident_np = np.eye(128, dtype=np.float32)
    ncores = int(os.environ.get("KN_CORES", "8"))
    in_maps = []
    for c in range(ncores):
        in_maps.append({"x": x[c], "wq": wq, "wk": wk, "wo": wo, "bo": bo,
                        "rw": rw, "w1": w1, "w2": w2, "ident_in": ident_np})
    res = run_bass_kernel_spmd(nc, in_maps, core_ids=list(range(ncores)))
    return np.stack([r["out"] for r in res.results], axis=0)


# version 14
# speedup vs baseline: 1.1044x; 1.1044x over previous
import os
import numpy as np
from contextlib import ExitStack

import concourse.bass as bass
import concourse.mybir as mybir
from concourse.bass_utils import run_bass_kernel_spmd

F32 = mybir.dt.float32
BF16 = mybir.dt.bfloat16
AF = mybir.ActivationFunctionType
ALU = mybir.AluOpType
AX = mybir.AxisListType

T = 4096
E = 1024
NPAIR = 8
HID = 4096
NEXP = 8
NT = T // 128
NTB = T // 512
NQ = 4
QTOK = T // NQ
NHG = 4
VS = 80

ENGINES = ("tensor", "scalar", "vector", "gpsimd", "sync")


class Builder:
    def __init__(self, nc):
        self.nc = nc
        self.ops = {e: [] for e in ENGINES}
        self.sems = {}
        self.counts = {}

    def make_sems(self, ctx, names):
        for n in names:
            self.sems[n] = ctx.enter_context(self.nc.semaphore(n))
            self.counts[n] = 0

    def emit(self, eng, fn, waits=(), inc=None, by=1):
        tok = None
        if inc is not None:
            self.counts[inc] += by
            tok = (inc, self.counts[inc])
        self.ops[eng].append((fn, [w for w in waits if w is not None], inc, by))
        return tok

    def snap(self, name):
        return (name, self.counts[name])

    def replay(self, eng, obj):
        seen = {}
        for fn, ws, inc, by in self.ops[eng]:
            need = {}
            for (s, v) in ws:
                if v > need.get(s, -1):
                    need[s] = v
            for s, v in need.items():
                if seen.get(s, 0) >= v:
                    continue
                obj.wait_ge(self.sems[s], v)
                seen[s] = v
            ins = fn(obj)
            if inc is not None:
                ins.then_inc(self.sems[inc], by)


def build_program(mode="full"):
    nc = bass.Bass("TRN2", target_bir_lowering=False)

    x_h = nc.dram_tensor("x", [T, E], F32, kind="ExternalInput").ap()
    wq_h = nc.dram_tensor("wq", [E, E], F32, kind="ExternalInput").ap()
    wk_h = nc.dram_tensor("wk", [E, E], F32, kind="ExternalInput").ap()
    wo_h = nc.dram_tensor("wo", [E, E], F32, kind="ExternalInput").ap()
    bo_h = nc.dram_tensor("bo", [128, E], F32, kind="ExternalInput").ap()
    rw_h = nc.dram_tensor("rw", [NEXP, E], F32, kind="ExternalInput").ap()
    w1_h = nc.dram_tensor("w1", [NEXP, E, HID], F32, kind="ExternalInput").ap()
    w2_h = nc.dram_tensor("w2", [NEXP, HID, E], F32, kind="ExternalInput").ap()
    id_h = nc.dram_tensor("ident_in", [128, 128], F32, kind="ExternalInput").ap()
    out_h = nc.dram_tensor("out", [T, E], F32, kind="ExternalOutput").ap()
    x1s_h = nc.dram_tensor("x1scratch", [T, E], F32, kind="Internal").ap()

    ctx = ExitStack()
    b = Builder(nc)
    em = b.emit
    b.make_sems(ctx, ["s_x", "s_w", "s_tr", "s_io", "s_st", "s_t", "s_sc", "s_v"])

    def sb(stack, name, shape, dt):
        return stack.enter_context(nc.sbuf_tensor(name, shape, dt))

    def psum(stack, name, shape, dt=F32):
        return stack.enter_context(nc.psum_tensor(name, shape, dt))

    xT = [sb(ctx, f"xT{cc}", [128, T], BF16) for cc in range(8)]
    bo_bc = sb(ctx, "bo_bc", [128, E], F32)
    comb_sb = sb(ctx, "comb_sb", [128, NT * 8], F32)
    epsc = sb(ctx, "epsc", [128, 1], F32)
    ident = sb(ctx, "ident", [128, 128], F32)

    scOT = ExitStack()
    OT = [sb(scOT, f"OT{dc}", [128, T], BF16) for dc in range(NPAIR)]

    scPair = ExitStack()
    QT = sb(scPair, "QT", [128, T], BF16)
    KT = sb(scPair, "KT", [128, T], BF16)
    Vp2 = [sb(scPair, f"Vp{i}", [128, 64 * VS], BF16) for i in range(2)]
    PT = [sb(scPair, f"PT{i}", [128, 1024], BF16) for i in range(2)]
    sqs = [sb(scPair, "sqs0", [128, 512], BF16)]
    Wqs = sb(scPair, "Wqs0", [128, 1024], BF16)
    Wks = sb(scPair, "Wks0", [128, 1024], BF16)
    xstage = [sb(scPair, f"xstage{i}", [128, E], F32) for i in range(2)]
    qparts = sb(scPair, "qparts", [128, 16], F32)
    normbuf = sb(scPair, "normbuf", [128, 8], F32)
    rinv_f = [sb(scPair, f"rinv_f{i}", [1, 512], BF16) for i in range(2)]
    rrow = [sb(scPair, f"rrow{i}", [1, 512], F32) for i in range(2)]
    rden_f = sb(scPair, "rden_f", [128, 512], BF16)
    ones1x64 = sb(scPair, "ones1x64", [1, 128], BF16)
    onesK = sb(scPair, "onesK", [128, 128], BF16)

    ps_s = [psum(scPair, f"ps_s{i}", [128, 1024]) for i in range(2)]
    ps_o = [psum(scPair, f"ps_o{i}", [128, 512]) for i in range(2)]
    ps_pr = [psum(scPair, f"ps_pr{i}", [128, 512]) for i in range(2)]

    t_bo = em("gpsimd", lambda g: g.dma_start(out=bo_bc[:, :], in_=bo_h[:, :]), inc="s_w", by=16)
    t_eps = em("vector", lambda v: v.memset(epsc[:, :], 1e-5), inc="s_v")
    t_o64 = em("vector", lambda v: v.memset(ones1x64[:, :], 1.0), inc="s_v")
    t_oK = em("vector", lambda v: v.memset(onesK[:, :], 1.0), inc="s_v")
    t_vp = [em("vector", lambda v, i=i: v.memset(Vp2[i][:, :], 1.0), inc="s_v") for i in range(2)]
    t_id = em("gpsimd", lambda g: g.dma_start(out=ident[:, :], in_=id_h[:, :]), inc="s_w", by=16)

    x_tr_tok = [None] * NT
    slot_free = [None, None]
    xstage_free = [None, None]
    for ti in range(NT):
        xb = ti % 2
        xl = em("sync", lambda s, ti=ti, xb=xb: s.dma_start(
            out=xstage[xb][:, :], in_=x_h[ti * 128:(ti + 1) * 128, :]),
            waits=[xstage_free[xb]], inc="s_io", by=16)
        for cc in range(8):
            sl = cc % 2
            col = 512 * sl
            trt = em("tensor", lambda t, cc=cc, col=col, xb=xb: t.transpose(
                ps_s[0][:, col:col + 128],
                xstage[xb][:, cc * 128:(cc + 1) * 128], ident[:, :]),
                waits=[xl, t_id, slot_free[sl]], inc="s_t")
            ev = em("vector", lambda v, ti=ti, cc=cc, col=col: v.tensor_copy(
                xT[cc][:, ti * 128:(ti + 1) * 128], ps_s[0][:, col:col + 128]),
                waits=[trt], inc="s_v")
            slot_free[sl] = ev
        x_tr_tok[ti] = slot_free[(8 - 1) % 2]
        xstage_free[xb] = b.snap("s_t")

    def xT_ready(tb):
        return x_tr_tok[tb * 4 + 3]

    wslice_tok = [None] * NPAIR
    proj_done_tok = [None] * NPAIR

    def emit_wslice(p):
        war = proj_done_tok[p - 1] if p >= 1 else None
        em("gpsimd", lambda g, p=p: g.dma_start(
            out=Wqs[:, :].rearrange("p (cc d) -> p cc d", d=128),
            in_=wq_h[:, p * 128:(p + 1) * 128].rearrange("(cc p) d -> p cc d", p=128)),
            waits=[war], inc="s_w", by=16)
        wslice_tok[p] = em("gpsimd", lambda g, p=p: g.dma_start(
            out=Wks[:, :].rearrange("p (cc d) -> p cc d", d=128),
            in_=wk_h[:, p * 128:(p + 1) * 128].rearrange("(cc p) d -> p cc d", p=128)),
            waits=[war], inc="s_w", by=16)

    emit_wslice(0)

    qt_free = None
    vp_free = [None, None]
    pt_free = [None, None]
    ps_s_free = [b.snap("s_v"), b.snap("s_v")]
    ps_o_free = [None, None]
    ps_pr_free = [None, None]
    sq_free = [None, None]
    rinv_free = [None, None]
    rrow_free = [None, None]
    last_exp_tok = None
    pending = [None]

    for p in range(NPAIR):
        pb = p % 2
        proj_last_mm = None
        vtr_toks = {}
        for (Wsl, TGT, isK) in ((Wks, KT, True), (Wqs, QT, False)):
            for tb in range(NTB):
                pp = tb % 2
                mmtok = None
                for cc in range(8):
                    w = []
                    if cc == 0:
                        w = [wslice_tok[p], xT_ready(tb), ps_pr_free[pp]]
                        if tb == 0 and isK:
                            w.append(qt_free)
                    mmtok = em("tensor", lambda t, Wsl=Wsl, cc=cc, tb=tb, pp=pp: t.matmul(
                        ps_pr[pp][:, :], Wsl[:, cc * 128:(cc + 1) * 128],
                        xT[cc][:, tb * 512:(tb + 1) * 512],
                        start=(cc == 0), stop=(cc == 7)),
                        waits=w, inc="s_t" if cc == 7 else None)
                proj_last_mm = mmtok
                evtok = em("vector", lambda v, TGT=TGT, tb=tb, pp=pp: v.tensor_copy(
                    TGT[:, tb * 512:(tb + 1) * 512], ps_pr[pp][:, :]),
                    waits=[mmtok], inc="s_v")
                ps_pr_free[pp] = evtok
                if isK:
                    gate = b.snap("s_tr")
                    for kc in range(4 * tb, 4 * tb + 4):
                        for h in range(2):
                            w = [evtok, gate]
                            if tb == 0 and h == 0 and kc == 0:
                                w += [vp_free[pb], t_vp[pb]]
                            last_vtr = em("sync", lambda s, pb=pb, h=h, kc=kc: s.dma_start_transpose(
                                Vp2[pb][:, (h * 32 + kc) * VS:(h * 32 + kc) * VS + 64],
                                KT[h * 64:(h + 1) * 64, kc * 128:(kc + 1) * 128]),
                                waits=w, inc="s_tr", by=16)
                    for kc in range(4 * tb, 4 * tb + 4):
                        for h in range(2):
                            vtr_toks[(h, kc)] = last_vtr
        proj_evac_done = ps_pr_free[(NTB - 1) % 2]
        proj_done_tok[p] = proj_last_mm
        if p + 1 < NPAIR:
            emit_wslice(p + 1)

        if pending[0] is not None:
            for c in pending[0]:
                c()
            pending[0] = None
        for (SRC, col) in ((QT, 0), (KT, 8)):
            for tb in range(NTB):
                sp = 0
                sqtok = em("vector", lambda v, SRC=SRC, tb=tb, sp=sp: v.tensor_tensor(
                    sqs[sp][:, :], SRC[:, tb * 512:(tb + 1) * 512],
                    SRC[:, tb * 512:(tb + 1) * 512], ALU.mult),
                    waits=[proj_evac_done, sq_free[sp]], inc="s_v")
                pp = tb % 2
                sq_free[1] = None
                qntok = em("tensor", lambda t, sp=sp, pp=pp: t.matmul(
                    ps_pr[pp][:, :], onesK[:, :], sqs[sp][:, :],
                    start=True, stop=True),
                    waits=[sqtok, t_oK, ps_pr_free[pp]], inc="s_t")
                sq_free[sp] = qntok
                ps_pr_free[pp] = em("vector", lambda v, col=col, tb=tb, pp=pp: v.reduce_max(
                    qparts[:, col + tb:col + tb + 1], ps_pr[pp][:, :], axis=AX.X),
                    waits=[qntok], inc="s_v")
        tq = em("vector", lambda v: v.reduce_max(
            normbuf[:, 0:1], qparts[:, 0:8], axis=AX.X),
            waits=[ps_pr_free[0], ps_pr_free[1], last_exp_tok], inc="s_v")
        tk2 = em("vector", lambda v: v.reduce_max(
            normbuf[:, 1:2], qparts[:, 8:16], axis=AX.X), waits=[tq], inc="s_v")
        biastok = em("vector", lambda v: v.tensor_scalar(
            normbuf[:, 2:3], normbuf[:, 0:1], normbuf[:, 1:2],
            -1.0 / 64.0, ALU.add, ALU.mult), waits=[tk2], inc="s_v")

        def make_chain(p, h, qb, evtoks):
            hb = h * 64

            def doit():
                rtok = evtoks[1]
                bctok = em("tensor", lambda t, h=h: t.matmul(
                    ps_pr[1][:, :], ones1x64[:, :], rinv_f[h][:, :], start=True, stop=True),
                    waits=[rtok, t_o64, ps_pr_free[1]], inc="s_t")
                rinv_free[h] = bctok
                cptok = em("vector", lambda v: v.tensor_copy(rden_f[:, :], ps_pr[1][:, :]),
                           waits=[bctok], inc="s_v")
                nm = em("vector", lambda v, p=p, hb=hb, qb=qb: v.tensor_tensor(
                    OT[p][hb:hb + 64, qb * 512:(qb + 1) * 512],
                    OT[p][hb:hb + 64, qb * 512:(qb + 1) * 512],
                    rden_f[hb:hb + 64, :], ALU.mult),
                    waits=[cptok], inc="s_v")
                ps_pr_free[1] = cptok
            return doit

        def emit_S(qb, kc):
            gb = kc % 2
            mm = None
            for h in range(2):
                hb = h * 64
                w = []
                if h == 0:
                    w = [ps_s_free[gb], biastok]
                mm = em("tensor", lambda t, hb=hb, qb=qb, kc=kc, h=h, gb=gb: t.matmul(
                    ps_s[gb][:, h * 512:(h + 1) * 512],
                    KT[hb:hb + 64, kc * 128:(kc + 1) * 128],
                    QT[hb:hb + 64, qb * 512:(qb + 1) * 512],
                    start=True, stop=True),
                    waits=w, inc="s_t" if h == 1 else None)
            return mm

        pv_last_h = [None, None]
        for qb in range(NTB):
            s_tok = emit_S(qb, 0)
            for kc in range(32):
                if kc == 8 and pending[0] is not None:
                    for c in pending[0]:
                        c()
                    pending[0] = None
                gb = kc % 2
                exptok = em("scalar", lambda sc, gb=gb: sc.activation(
                    PT[gb][:, :], ps_s[gb][:, :], AF.Exp,
                    bias=normbuf[:, 2:3], scale=1.0 / 32.0),
                    waits=[s_tok, pt_free[gb]], inc="s_sc")
                ps_s_free[gb] = exptok
                last_exp_tok = exptok
                if kc + 1 < 32:
                    s_tok = emit_S(qb, kc + 1)
                for h in range(2):
                    w = [exptok, vtr_toks[(h, kc)]]
                    if kc == 0:
                        w += [ps_o_free[h]]
                    pv_last_h[h] = em("tensor", lambda t, pb=pb, h=h, kc=kc, gb=gb: t.matmul(
                        ps_o[h][0:65, :],
                        Vp2[pb][:, (h * 32 + kc) * VS:(h * 32 + kc) * VS + 65],
                        PT[gb][:, h * 512:(h + 1) * 512],
                        start=(kc == 0), stop=(kc == 31)),
                        waits=w, inc="s_t")
                pt_free[gb] = pv_last_h[1]
            chains = []
            evs = []
            for h in range(2):
                hb = h * 64
                evo = em("vector", lambda v, p=p, hb=hb, qb=qb, h=h: v.tensor_copy(
                    OT[p][hb:hb + 64, qb * 512:(qb + 1) * 512], ps_o[h][0:64, :]),
                    waits=[pv_last_h[h]], inc="s_v")
                evr = em("vector", lambda v, h=h: v.tensor_copy(
                    rrow[h][:, :], ps_o[h][64:65, :]),
                    waits=[evo, rrow_free[h]], inc="s_v")
                ps_o_free[h] = evr
                evs.append((evo, evr))
            for h in range(2):
                def _rc(v, h=h):
                    with nc.allow_low_precision(reason="softmax 1/r broadcast operand is bf16"):
                        return v.reciprocal(rinv_f[h][:, :], rrow[h][:, :])
                rc = em("vector", _rc, waits=[evs[h][1], rinv_free[h]], inc="s_v")
                rrow_free[h] = rc
                chains.append(make_chain(p, h, qb, [evs[h][0], rc]))
            pending[0] = chains
        vp_free[pb] = pv_last_h[1]
        qt_free = pv_last_h[1]

    if pending[0] is not None:
        for c in pending[0]:
            c()
        pending[0] = None
    barrA = [b.snap("s_t"), b.snap("s_sc"), b.snap("s_v"), b.snap("s_tr"), b.snap("s_io")]

    scPair.close()

    NB = 4
    scOP = ExitStack()
    Wo_bf = sb(scOP, "Wo_bf", [128, 8192], BF16)
    xres = [sb(scOP, f"xres{i}", [128, E], F32) for i in range(NB)]
    ubuf = [sb(scOP, f"ubuf{i}", [128, E], F32) for i in range(NB)]
    usq_dump = sb(scOP, "usq_dump", [128, E], BF16)
    x1f = [sb(scOP, f"x1f{i}", [128, E], F32) for i in range(NB)]
    stats = sb(scOP, "stats", [128, 16 * NB], F32)
    ps_op = [psum(scOP, f"ps_op{i}", [128, 1024]) for i in range(3)]
    ps_t1 = [psum(scOP, f"ps_t1{i}", [128, 128]) for i in range(2)]

    wo_tok = em("gpsimd", lambda g: g.dma_start(
        out=Wo_bf[:, :].rearrange("p (dc c) -> p dc c", c=1024),
        in_=wo_h[:, :].rearrange("(dc p) c -> p dc c", p=128)),
        waits=barrA, inc="s_w", by=16)

    x1_tr_tok = [None] * NT
    trq = []
    ps_op_free = [None] * 3
    ubuf_free = [None] * NB
    x1f_free = [[], [], [], []]
    t1_free = [None, None]
    st_total = 0

    for ti in range(NT):
        tp = ti % NB
        tq_ = ti % 3
        so = 16 * tp
        xl = em("sync", lambda s, ti=ti, tp=tp: s.dma_start(
            out=xres[tp][:, :], in_=x_h[ti * 128:(ti + 1) * 128, :]),
            waits=barrA + [ubuf_free[tp]], inc="s_io", by=16)
        mmtok = None
        for cc2 in range(2):
            for dc in range(8):
                w = []
                if cc2 == 0 and dc == 0:
                    w = barrA + [wo_tok, ps_op_free[tq_]]
                mmtok = em("tensor", lambda t, ti=ti, tp=tq_, cc2=cc2, dc=dc: t.matmul(
                    ps_op[tp][:, cc2 * 512:(cc2 + 1) * 512],
                    OT[dc][:, ti * 128:(ti + 1) * 128],
                    Wo_bf[:, dc * 1024 + cc2 * 512:dc * 1024 + (cc2 + 1) * 512],
                    start=(dc == 0), stop=(dc == 7)),
                    waits=w, inc="s_t" if (dc == 7 and cc2 == 1) else None)
        ut1 = em("vector", lambda v, tp=tp: v.tensor_tensor(
            ubuf[tp][:, :], xres[tp][:, :], bo_bc[:, :], ALU.add),
            waits=[xl, t_bo] + x1f_free[tp], inc="s_v")
        ut2 = em("vector", lambda v, tp=tp, tq_=tq_: v.tensor_tensor(
            ubuf[tp][:, :], ubuf[tp][:, :], ps_op[tq_][:, :], ALU.add),
            waits=[ut1, mmtok], inc="s_v")
        ps_op_free[tq_] = ut2
        sstok = em("scalar", lambda sc, tp=tp, so=so: sc.activation(
            usq_dump[:, :], ubuf[tp][:, :], AF.Square, accum_out=stats[:, so:so + 1]),
            waits=[ut2], inc="s_sc")
        s1tok = em("vector", lambda v, tp=tp, so=so: v.reduce_sum(
            stats[:, so + 1:so + 2], ubuf[tp][:, :], axis=AX.X), waits=[ut2], inc="s_v")
        smtok = em("vector", lambda v, so=so: v.tensor_scalar(
            stats[:, so + 5:so + 6], stats[:, so + 1:so + 2], 1.0 / E, None, ALU.mult),
            waits=[s1tok], inc="s_v")
        sm2tok = em("vector", lambda v, so=so: v.tensor_scalar(
            stats[:, so + 6:so + 7], stats[:, so + 5:so + 6], stats[:, so + 5:so + 6],
            None, ALU.mult), waits=[smtok], inc="s_v")
        vartok = em("vector", lambda v, so=so: v.tensor_scalar(
            stats[:, so + 7:so + 8], stats[:, so:so + 1], 1.0 / E,
            stats[:, so + 6:so + 7], ALU.mult, ALU.subtract),
            waits=[sm2tok], inc="s_v")
        lntok = em("scalar", lambda sc, so=so: sc.activation(
            stats[:, so + 8:so + 9], stats[:, so + 7:so + 8], AF.Ln, bias=epsc[:, 0:1]),
            waits=[vartok, t_eps], inc="s_sc")
        rstok = em("scalar", lambda sc, so=so: sc.activation(
            stats[:, so + 9:so + 10], stats[:, so + 8:so + 9], AF.Exp, scale=-0.5),
            waits=[lntok], inc="s_sc")
        x1tok = em("vector", lambda v, tp=tp, so=so: v.tensor_scalar(
            x1f[tp][:, :], ubuf[tp][:, :], stats[:, so + 5:so + 6],
            stats[:, so + 9:so + 10], ALU.subtract, ALU.mult),
            waits=[rstok], inc="s_v")
        ubuf_free[tp] = x1tok
        x1st = em("sync", lambda s, ti=ti, tp=tp: s.dma_start(
            out=x1s_h[ti * 128:(ti + 1) * 128, :], in_=x1f[tp][:, :]),
            waits=[x1tok], inc="s_io", by=16)
        frees = [x1st]
        if mode == "ln1":
            frees.append(em("sync", lambda s, ti=ti, tp=tp: s.dma_start(
                out=out_h[ti * 128:(ti + 1) * 128, :], in_=x1f[tp][:, :]),
                waits=[x1tok], inc="s_st", by=16))
            st_total += 16
        def emit_x1tr(ti, tp, x1tok, frees):
            for cc in range(8):
                sl = cc % 2
                trt = em("tensor", lambda t, tp=tp, cc=cc, sl=sl: t.transpose(
                    ps_t1[sl][:, :], x1f[tp][:, cc * 128:(cc + 1) * 128], ident[:, :]),
                    waits=[x1tok, t1_free[sl]], inc="s_t")
                t1_free[sl] = em("vector", lambda v, ti=ti, cc=cc, sl=sl: v.tensor_copy(
                    xT[cc][:, ti * 128:(ti + 1) * 128], ps_t1[sl][:, :]),
                    waits=[trt], inc="s_v")
            x1_tr_tok[ti] = t1_free[(8 - 1) % 2]
            frees.append(b.snap("s_t"))
            x1f_free[tp] = frees
        trq.append((ti, tp, x1tok, frees))
        if len(trq) > 3:
            emit_x1tr(*trq.pop(0))
        if ti == NT - 1:
            while trq:
                emit_x1tr(*trq.pop(0))

    if mode == "ln1":
        em("vector", lambda v: v.memset(stats[:, 15:16], 0.0),
           waits=[("s_st", st_total)], inc="s_v")
        blk = ctx.enter_context(nc.Block())
        for eng in ENGINES:
            getattr(blk, eng)(lambda obj, e=eng: b.replay(e, obj))
        scOP.close()
        scOT.close()
        ctx.close()
        return nc

    barrOP = [b.snap("s_t"), b.snap("s_sc"), b.snap("s_v"), b.snap("s_io")]
    scOP.close()

    scR = ExitStack()
    rw_bf = sb(scR, "rw_bf", [16, E], BF16)
    rwT = sb(scR, "rwT", [128, 128], BF16)
    lg = sb(scR, "lg", [128, 256], F32)
    Ee = sb(scR, "Ee", [128, 256], F32)
    msk = sb(scR, "msk", [128, 256], F32)
    gee = sb(scR, "gee", [128, 256], F32)
    m1b = sb(scR, "m1b", [128, 32], F32)
    m2b = sb(scR, "m2b", [128, 32], F32)
    s12 = sb(scR, "s12", [128, 32], F32)
    ri32 = sb(scR, "ri32", [128, 32], F32)
    m20c = sb(scR, "m20c", [128, 1], F32)
    ps_r = [psum(scR, f"ps_r{i}", [128, 8]) for i in range(2)]

    def g3(ap):
        return ap.rearrange("p (t e) -> p t e", e=8)

    def b3(ap):
        return ap.rearrange("p (t o) -> p t o", o=1).broadcast_to([128, 32, 8])

    rw_ld = em("gpsimd", lambda g: g.dma_start(out=rw_bf[0:8, :], in_=rw_h[:, :]),
               waits=barrOP, inc="s_w", by=16)
    t_m20 = em("vector", lambda v: v.memset(m20c[:, :], -20.0), inc="s_v")
    rwt_tok = None
    for cc in range(8):
        rwt_tok = em("sync", lambda s, cc=cc: s.dma_start_transpose(
            rwT[:, cc * 16:(cc + 1) * 16], rw_bf[:, cc * 128:(cc + 1) * 128]),
            waits=[rw_ld], inc="s_tr", by=16)

    ps_r_free = [None, None]
    ev_last = None
    for ti in range(NT):
        rb = ti % 2
        mmtok = None
        for cc in range(8):
            w = []
            if cc == 0:
                w = [rwt_tok, x1_tr_tok[ti], ps_r_free[rb]] + (barrOP if ti == 0 else [])
            mmtok = em("tensor", lambda t, ti=ti, cc=cc, rb=rb: t.matmul(
                ps_r[rb][:, :], xT[cc][:, ti * 128:(ti + 1) * 128], rwT[:, cc * 16:cc * 16 + 8],
                start=(cc == 0), stop=(cc == 7)), waits=w, inc="s_t" if cc == 7 else None)
        ev_last = em("vector", lambda v, ti=ti, rb=rb: v.tensor_copy(
            lg[:, ti * 8:(ti + 1) * 8], ps_r[rb][:, :]), waits=[mmtok], inc="s_v")
        ps_r_free[rb] = ev_last
    ex = em("scalar", lambda sc: sc.activation(Ee[:, :], lg[:, :], AF.Exp, bias=m20c[:, 0:1]),
            waits=[ev_last, t_m20], inc="s_sc")
    c1 = em("vector", lambda v: v.reduce_max(m1b[:, :], g3(Ee[:, :]), axis=AX.X),
            waits=[ex], inc="s_v")
    c2 = em("vector", lambda v: v.tensor_tensor(g3(msk[:, :]), g3(Ee[:, :]), b3(m1b[:, :]),
                                                ALU.is_ge), waits=[c1], inc="s_v")
    c3 = em("vector", lambda v: v.scalar_tensor_tensor(msk[:, :], msk[:, :], -1e30,
                                                       Ee[:, :], ALU.mult, ALU.add),
            waits=[c2], inc="s_v")
    c4 = em("vector", lambda v: v.reduce_max(m2b[:, :], g3(msk[:, :]), axis=AX.X),
            waits=[c3], inc="s_v")
    c5 = em("vector", lambda v: v.tensor_tensor(g3(gee[:, :]), g3(Ee[:, :]), b3(m2b[:, :]),
                                                ALU.is_ge), waits=[c4], inc="s_v")
    c6 = em("vector", lambda v: v.tensor_tensor(s12[:, :], m1b[:, :], m2b[:, :], ALU.add),
            waits=[c4], inc="s_v")
    c7 = em("vector", lambda v: v.reciprocal(ri32[:, :], s12[:, :]), waits=[c6], inc="s_v")
    c8 = em("vector", lambda v: v.tensor_tensor(gee[:, :], gee[:, :], Ee[:, :], ALU.mult),
            waits=[c5], inc="s_v")
    em("vector", lambda v: v.tensor_tensor(g3(comb_sb[:, :]), g3(gee[:, :]), b3(ri32[:, :]),
                                           ALU.mult), waits=[c7, c8], inc="s_v")
    comb_done = b.snap("s_v")

    barrB = [b.snap("s_t"), b.snap("s_sc"), b.snap("s_v"), b.snap("s_tr"), b.snap("s_io")]
    scR.close()
    scOT.close()

    scF = ExitStack()
    acc = sb(scF, "acc", [128, 8 * E], F32)
    w1b = [sb(scF, f"w1b{i}", [128, 8192], BF16) for i in range(2)]
    w2b = [sb(scF, f"w2b{i}", [128, 8192], BF16) for i in range(2)]
    hT = [sb(scF, f"hT{i}", [128, 512], BF16) for i in range(8)]
    x1res = [sb(scF, f"x1res{i}", [128, E], F32) for i in range(2)]
    u2buf = [sb(scF, f"u2buf{i}", [128, E], F32) for i in range(2)]
    o2buf = [sb(scF, f"o2buf{i}", [128, E], F32) for i in range(2)]
    usq2 = sb(scF, "usq2", [128, E], BF16)
    stats2 = sb(scF, "stats2", [128, 32], F32)
    ps_h = [psum(scF, f"ps_h{i}", [128, 512]) for i in range(2)]
    ps_y = [psum(scF, f"ps_y{i}", [128, 512]) for i in range(2)]

    NSLICE = NQ * NEXP * NHG
    wready = [None] * NSLICE
    wfree = [None] * NSLICE

    def emit_wdma(s):
        q, rem = divmod(s, NEXP * NHG)
        e, hg = divmod(rem, NHG)
        buf = s % 2
        war = wfree[s - 2] if s >= 2 else None
        w = [war] + (barrB if s < 2 else [])
        em("gpsimd", lambda g, e=e, hg=hg, buf=buf: g.dma_start(
            out=w1b[buf][:, :].rearrange("p (cc h) -> p cc h", h=1024),
            in_=w1_h[e, :, hg * 1024:(hg + 1) * 1024].rearrange("(cc p) h -> p cc h", p=128)),
            waits=w, inc="s_w", by=16)
        wready[s] = em("gpsimd", lambda g, e=e, hg=hg, buf=buf: g.dma_start(
            out=w2b[buf][:, :].rearrange("p (hc c) -> p hc c", c=1024),
            in_=w2_h[e, hg * 1024:(hg + 1) * 1024, :].rearrange("(hc p) c -> p hc c", p=128)),
            waits=w, inc="s_w", by=16)

    emit_wdma(0)
    emit_wdma(1)

    ps_h_free = [None, None]
    ps_y_free = [None, None]
    st_total = 0
    x1res_free = [None, None]
    u2_free = [None, None]
    o2_free = [None, None]

    for q in range(NQ):
        for e in range(NEXP):
            for hg in range(NHG):
                s = (q * NEXP + e) * NHG + hg
                buf = s % 2
                wlast = None
                for tb in range(2):
                    tok0 = q * QTOK + tb * 512
                    gelu_toks = []
                    for hc in range(8):
                        hb2 = hc % 2
                        mmtok = None
                        for cc in range(8):
                            w = []
                            if cc == 0:
                                w = [wready[s], ps_h_free[hb2]]
                                if e == 0 and hg == 0 and tb == 0 and hc == 0:
                                    w = w + [x1_tr_tok[q * 8 + 7]]
                                if s < 2 and tb == 0 and hc == 0:
                                    w = w + barrB
                            mmtok = em("tensor", lambda t, buf=buf, cc=cc, hc=hc, tok0=tok0, hb2=hb2: t.matmul(
                                ps_h[hb2][:, :],
                                w1b[buf][:, cc * 1024 + hc * 128:cc * 1024 + (hc + 1) * 128],
                                xT[cc][:, tok0:tok0 + 512],
                                start=(cc == 0), stop=(cc == 7)),
                                waits=w, inc="s_t" if cc == 7 else None)
                        gt = em("scalar", lambda sc, hc=hc, hb2=hb2: sc.activation(
                            hT[hc][:, :], ps_h[hb2][:, :], AF.Gelu), waits=[mmtok], inc="s_sc")
                        ps_h_free[hb2] = gt
                        gelu_toks.append(gt)
                    for tt2 in range(4):
                        ti_q = tb * 4 + tt2
                        tg = q * 8 + ti_q
                        for cc2 in range(2):
                            yb = (tt2 * 2 + cc2) % 2
                            mmtok = None
                            for hc in range(8):
                                w = [gelu_toks[hc]]
                                if hc == 0:
                                    w.append(ps_y_free[yb])
                                mmtok = em("tensor", lambda t, buf=buf, hc=hc, tt2=tt2, cc2=cc2, yb=yb: t.matmul(
                                    ps_y[yb][:, :],
                                    hT[hc][:, tt2 * 128:(tt2 + 1) * 128],
                                    w2b[buf][:, hc * 1024 + cc2 * 512:hc * 1024 + (cc2 + 1) * 512],
                                    start=(hc == 0), stop=(hc == 7)),
                                    waits=w, inc="s_t" if hc == 7 else None)
                            wlast = mmtok
                            if e == 0 and hg == 0:
                                ev = em("vector", lambda v, ti_q=ti_q, cc2=cc2, yb=yb, tg=tg, e=e: v.tensor_scalar(
                                    acc[:, ti_q * 1024 + cc2 * 512:ti_q * 1024 + (cc2 + 1) * 512],
                                    ps_y[yb][:, :], comb_sb[:, tg * 8 + e:tg * 8 + e + 1],
                                    None, ALU.mult),
                                    waits=[mmtok, comb_done if s == 0 else None], inc="s_v")
                            else:
                                ev = em("vector", lambda v, ti_q=ti_q, cc2=cc2, yb=yb, tg=tg, e=e: v.scalar_tensor_tensor(
                                    acc[:, ti_q * 1024 + cc2 * 512:ti_q * 1024 + (cc2 + 1) * 512],
                                    ps_y[yb][:, :], comb_sb[:, tg * 8 + e:tg * 8 + e + 1],
                                    acc[:, ti_q * 1024 + cc2 * 512:ti_q * 1024 + (cc2 + 1) * 512],
                                    ALU.mult, ALU.add), waits=[mmtok], inc="s_v")
                            ps_y_free[yb] = ev
                wfree[s] = wlast
                if s + 2 < NSLICE:
                    emit_wdma(s + 2)
        for ti_q in range(8):
            ti = q * 8 + ti_q
            tp = ti_q % 2
            so = 16 * tp
            xl = em("sync", lambda sy, ti=ti, tp=tp: sy.dma_start(
                out=x1res[tp][:, :], in_=x1s_h[ti * 128:(ti + 1) * 128, :]),
                waits=x1f_free[0] + x1f_free[1] + x1f_free[2] + x1f_free[3] + [x1res_free[tp]],
                inc="s_io", by=16)
            ut = em("vector", lambda v, ti_q=ti_q, tp=tp: v.tensor_tensor(
                u2buf[tp][:, :], acc[:, ti_q * 1024:(ti_q + 1) * 1024], x1res[tp][:, :], ALU.add),
                waits=[xl, ps_y_free[0], ps_y_free[1], u2_free[tp]], inc="s_v")
            x1res_free[tp] = ut
            sstok = em("scalar", lambda sc, tp=tp, so=so: sc.activation(
                usq2[:, :], u2buf[tp][:, :], AF.Square, accum_out=stats2[:, so:so + 1]),
                waits=[ut], inc="s_sc")
            s1tok = em("scalar", lambda sc, tp=tp, so=so: sc.activation(
                usq2[:, :], u2buf[tp][:, :], AF.Copy, accum_out=stats2[:, so + 1:so + 2]),
                waits=[sstok], inc="s_sc")
            smtok = em("vector", lambda v, so=so: v.tensor_scalar(
                stats2[:, so + 5:so + 6], stats2[:, so + 1:so + 2], 1.0 / E, None,
                ALU.mult), waits=[s1tok], inc="s_v")
            sm2tok = em("vector", lambda v, so=so: v.tensor_scalar(
                stats2[:, so + 6:so + 7], stats2[:, so + 5:so + 6],
                stats2[:, so + 5:so + 6], None, ALU.mult), waits=[smtok], inc="s_v")
            vartok = em("vector", lambda v, so=so: v.tensor_scalar(
                stats2[:, so + 7:so + 8], stats2[:, so:so + 1],
                1.0 / E, stats2[:, so + 6:so + 7], ALU.mult, ALU.subtract),
                waits=[sm2tok], inc="s_v")
            lntok = em("scalar", lambda sc, so=so: sc.activation(
                stats2[:, so + 8:so + 9], stats2[:, so + 7:so + 8], AF.Ln, bias=epsc[:, 0:1]),
                waits=[vartok, t_eps], inc="s_sc")
            rstok = em("scalar", lambda sc, so=so: sc.activation(
                stats2[:, so + 9:so + 10], stats2[:, so + 8:so + 9], AF.Exp, scale=-0.5),
                waits=[lntok], inc="s_sc")
            otok = em("vector", lambda v, tp=tp, so=so: v.tensor_scalar(
                o2buf[tp][:, :], u2buf[tp][:, :], stats2[:, so + 5:so + 6],
                stats2[:, so + 9:so + 10], ALU.subtract, ALU.mult),
                waits=[rstok, o2_free[tp]], inc="s_v")
            u2_free[tp] = otok
            sttok = em("sync", lambda sy, ti=ti, tp=tp: sy.dma_start(
                out=out_h[ti * 128:(ti + 1) * 128, :], in_=o2buf[tp][:, :]),
                waits=[otok], inc="s_st", by=16)
            o2_free[tp] = sttok
            st_total += 16

    em("vector", lambda v: v.memset(stats2[:, 15:16], 0.0),
       waits=[("s_st", st_total)], inc="s_v")

    blk = ctx.enter_context(nc.Block())
    for eng in ENGINES:
        getattr(blk, eng)(lambda obj, e=eng: b.replay(e, obj))
    scF.close()
    ctx.close()
    return nc


_prog_cache = {}


def _get_prog(mode):
    if mode not in _prog_cache:
        _prog_cache[mode] = build_program(mode)
    return _prog_cache[mode]


def kernel(L0_attn_x, L0_attn_Wk, L0_attn_Wq, L0_attn_Wo, L0_attn_bo,
           L0_ffn_router_w, L0_ffn_w1, L0_ffn_w2, mode="full"):
    x = np.ascontiguousarray(np.asarray(L0_attn_x, dtype=np.float32))
    wk = np.ascontiguousarray(np.asarray(L0_attn_Wk, dtype=np.float32))
    wq = np.ascontiguousarray(np.asarray(L0_attn_Wq, dtype=np.float32))
    wo = np.ascontiguousarray(np.asarray(L0_attn_Wo, dtype=np.float32))
    bo = np.ascontiguousarray(np.broadcast_to(
        np.asarray(L0_attn_bo, dtype=np.float32).reshape(1, E), (128, E)))
    rw = np.ascontiguousarray(np.asarray(L0_ffn_router_w, dtype=np.float32))
    w1 = np.ascontiguousarray(np.asarray(L0_ffn_w1, dtype=np.float32))
    w2 = np.ascontiguousarray(np.asarray(L0_ffn_w2, dtype=np.float32))

    nc = _get_prog(mode)
    ident_np = np.eye(128, dtype=np.float32)
    ncores = int(os.environ.get("KN_CORES", "8"))
    in_maps = []
    for c in range(ncores):
        in_maps.append({"x": x[c], "wq": wq, "wk": wk, "wo": wo, "bo": bo,
                        "rw": rw, "w1": w1, "w2": w2, "ident_in": ident_np})
    res = run_bass_kernel_spmd(nc, in_maps, core_ids=list(range(ncores)))
    return np.stack([r["out"] for r in res.results], axis=0)


# version 16 (final)
# speedup vs baseline: 1.2519x; 1.1336x over previous
import os
import numpy as np
from contextlib import ExitStack

import concourse.bass as bass
import concourse.mybir as mybir
from concourse.bass_utils import run_bass_kernel_spmd

F32 = mybir.dt.float32
BF16 = mybir.dt.bfloat16
AF = mybir.ActivationFunctionType
ALU = mybir.AluOpType
AX = mybir.AxisListType

T = 4096
E = 1024
NPAIR = 8
HID = 4096
NEXP = 8
NT = T // 128
NTB = T // 512
NQ = 4
QTOK = T // NQ
NHG = 4
VS = 80

ENGINES = ("tensor", "scalar", "vector", "gpsimd", "sync")


class Builder:
    def __init__(self, nc):
        self.nc = nc
        self.ops = {e: [] for e in ENGINES}
        self.sems = {}
        self.counts = {}

    def make_sems(self, ctx, names):
        for n in names:
            self.sems[n] = ctx.enter_context(self.nc.semaphore(n))
            self.counts[n] = 0

    def emit(self, eng, fn, waits=(), inc=None, by=1):
        tok = None
        if inc is not None:
            self.counts[inc] += by
            tok = (inc, self.counts[inc])
        self.ops[eng].append((fn, [w for w in waits if w is not None], inc, by))
        return tok

    def snap(self, name):
        return (name, self.counts[name])

    def replay(self, eng, obj):
        seen = {}
        for fn, ws, inc, by in self.ops[eng]:
            need = {}
            for (s, v) in ws:
                if v > need.get(s, -1):
                    need[s] = v
            for s, v in need.items():
                if seen.get(s, 0) >= v:
                    continue
                obj.wait_ge(self.sems[s], v)
                seen[s] = v
            ins = fn(obj)
            if inc is not None:
                ins.then_inc(self.sems[inc], by)


def build_program(mode="full"):
    nc = bass.Bass("TRN2", target_bir_lowering=False)

    x_h = nc.dram_tensor("x", [T, E], F32, kind="ExternalInput").ap()
    wq_h = nc.dram_tensor("wq", [E, E], F32, kind="ExternalInput").ap()
    wk_h = nc.dram_tensor("wk", [E, E], F32, kind="ExternalInput").ap()
    wo_h = nc.dram_tensor("wo", [E, E], F32, kind="ExternalInput").ap()
    bo_h = nc.dram_tensor("bo", [128, E], F32, kind="ExternalInput").ap()
    rw_h = nc.dram_tensor("rw", [NEXP, E], F32, kind="ExternalInput").ap()
    w1_h = nc.dram_tensor("w1", [NEXP, E, HID], F32, kind="ExternalInput").ap()
    w2_h = nc.dram_tensor("w2", [NEXP, HID, E], F32, kind="ExternalInput").ap()
    id_h = nc.dram_tensor("ident_in", [128, 128], F32, kind="ExternalInput").ap()
    out_h = nc.dram_tensor("out", [T, E], F32, kind="ExternalOutput").ap()
    x1s_h = nc.dram_tensor("x1scratch", [T, E], F32, kind="Internal").ap()

    ctx = ExitStack()
    b = Builder(nc)
    em = b.emit
    b.make_sems(ctx, ["s_w", "s_tr", "s_io", "s_st", "s_t", "s_sc", "s_v"])

    def sb(stack, name, shape, dt):
        return stack.enter_context(nc.sbuf_tensor(name, shape, dt))

    def psum(stack, name, shape, dt=F32):
        return stack.enter_context(nc.psum_tensor(name, shape, dt))

    xT = [sb(ctx, f"xT{cc}", [128, T], BF16) for cc in range(8)]
    bo_bc = sb(ctx, "bo_bc", [128, E], F32)
    comb_sb = sb(ctx, "comb_sb", [128, NT * 8], F32)
    epsc = sb(ctx, "epsc", [128, 1], F32)
    ident = sb(ctx, "ident", [128, 128], F32)

    scOT = ExitStack()
    OT = [sb(scOT, f"OT{dc}", [128, T], BF16) for dc in range(NPAIR)]

    scPair = ExitStack()
    QT = sb(scPair, "QT", [128, T], BF16)
    KT = sb(scPair, "KT", [128, T], BF16)
    Vp2 = [sb(scPair, f"Vp{i}", [128, 64 * VS], BF16) for i in range(2)]
    PT = [sb(scPair, f"PT{i}", [128, 1024], BF16) for i in range(2)]
    sqs = [sb(scPair, "sqs0", [128, 512], BF16)]
    Wqs = sb(scPair, "Wqs0", [128, 1024], BF16)
    Wks = sb(scPair, "Wks0", [128, 1024], BF16)
    xstage = [sb(scPair, f"xstage{i}", [128, E], F32) for i in range(2)]
    qparts = sb(scPair, "qparts", [128, 16], F32)
    normbuf = sb(scPair, "normbuf", [128, 8], F32)
    rinv_f = [sb(scPair, f"rinv_f{i}", [1, 512], BF16) for i in range(2)]
    rrow = [sb(scPair, f"rrow{i}", [1, 512], F32) for i in range(2)]
    rden_f = sb(scPair, "rden_f", [128, 512], BF16)
    ones1x64 = sb(scPair, "ones1x64", [1, 128], BF16)
    onesK = sb(scPair, "onesK", [128, 128], BF16)

    ps_s = [psum(scPair, f"ps_s{i}", [128, 1024]) for i in range(2)]
    ps_o = [psum(scPair, f"ps_o{i}", [128, 512]) for i in range(2)]
    ps_pr = [psum(scPair, f"ps_pr{i}", [128, 512]) for i in range(2)]

    t_bo = em("gpsimd", lambda g: g.dma_start(out=bo_bc[:, :], in_=bo_h[:, :]), inc="s_w", by=16)
    t_eps = em("vector", lambda v: v.memset(epsc[:, :], 1e-5), inc="s_v")
    t_o64 = em("vector", lambda v: v.memset(ones1x64[:, :], 1.0), inc="s_v")
    t_oK = em("vector", lambda v: v.memset(onesK[:, :], 1.0), inc="s_v")
    t_vp = [em("vector", lambda v, i=i: v.memset(Vp2[i][:, :], 1.0), inc="s_v") for i in range(2)]
    t_id = em("gpsimd", lambda g: g.dma_start(out=ident[:, :], in_=id_h[:, :]), inc="s_w", by=16)

    x_tr_tok = [None] * NT
    slot_free = [None, None]
    xstage_free = [None, None]
    for ti in range(NT):
        xb = ti % 2
        xl = em("sync", lambda s, ti=ti, xb=xb: s.dma_start(
            out=xstage[xb][:, :], in_=x_h[ti * 128:(ti + 1) * 128, :]),
            waits=[xstage_free[xb]], inc="s_io", by=16)
        for cc in range(8):
            sl = cc % 2
            col = 512 * sl
            trt = em("tensor", lambda t, cc=cc, col=col, xb=xb: t.transpose(
                ps_s[0][:, col:col + 128],
                xstage[xb][:, cc * 128:(cc + 1) * 128], ident[:, :]),
                waits=[xl, t_id, slot_free[sl]], inc="s_t")
            ev = em("vector", lambda v, ti=ti, cc=cc, col=col: v.tensor_copy(
                xT[cc][:, ti * 128:(ti + 1) * 128], ps_s[0][:, col:col + 128]),
                waits=[trt], inc="s_v")
            slot_free[sl] = ev
        x_tr_tok[ti] = slot_free[(8 - 1) % 2]
        xstage_free[xb] = b.snap("s_t")

    def xT_ready(tb):
        return x_tr_tok[tb * 4 + 3]

    wslice_tok = [None] * NPAIR
    proj_done_tok = [None] * NPAIR

    def emit_wslice(p):
        war = proj_done_tok[p - 1] if p >= 1 else None
        em("gpsimd", lambda g, p=p: g.dma_start(
            out=Wqs[:, :].rearrange("p (cc d) -> p cc d", d=128),
            in_=wq_h[:, p * 128:(p + 1) * 128].rearrange("(cc p) d -> p cc d", p=128)),
            waits=[war], inc="s_w", by=16)
        wslice_tok[p] = em("gpsimd", lambda g, p=p: g.dma_start(
            out=Wks[:, :].rearrange("p (cc d) -> p cc d", d=128),
            in_=wk_h[:, p * 128:(p + 1) * 128].rearrange("(cc p) d -> p cc d", p=128)),
            waits=[war], inc="s_w", by=16)

    emit_wslice(0)

    qt_free = None
    vp_free = [None, None]
    pt_free = [None, None]
    ps_s_free = [b.snap("s_v"), b.snap("s_v")]
    ps_o_free = [None, None]
    ps_pr_free = [None, None]
    sq_free = [None, None]
    rinv_free = [None, None]
    rrow_free = [None, None]
    last_exp_tok = None
    pending = [None]

    for p in range(NPAIR):
        pb = p % 2
        proj_last_mm = None
        vtr_toks = {}
        for (Wsl, TGT, isK) in ((Wks, KT, True), (Wqs, QT, False)):
            for tb in range(NTB):
                pp = tb % 2
                mmtok = None
                for cc in range(8):
                    w = []
                    if cc == 0:
                        w = [wslice_tok[p], xT_ready(tb), ps_pr_free[pp]]
                        if tb == 0 and isK:
                            w.append(qt_free)
                    mmtok = em("tensor", lambda t, Wsl=Wsl, cc=cc, tb=tb, pp=pp: t.matmul(
                        ps_pr[pp][:, :], Wsl[:, cc * 128:(cc + 1) * 128],
                        xT[cc][:, tb * 512:(tb + 1) * 512],
                        start=(cc == 0), stop=(cc == 7)),
                        waits=w, inc="s_t" if cc == 7 else None)
                proj_last_mm = mmtok
                evtok = em("vector", lambda v, TGT=TGT, tb=tb, pp=pp: v.tensor_copy(
                    TGT[:, tb * 512:(tb + 1) * 512], ps_pr[pp][:, :]),
                    waits=[mmtok], inc="s_v")
                ps_pr_free[pp] = evtok
                if isK:
                    gate = b.snap("s_tr")
                    for kc in range(4 * tb, 4 * tb + 4):
                        for h in range(2):
                            w = [evtok, gate]
                            if tb == 0 and h == 0 and kc == 0:
                                w += [vp_free[pb], t_vp[pb]]
                            last_vtr = em("sync", lambda s, pb=pb, h=h, kc=kc: s.dma_start_transpose(
                                Vp2[pb][:, (h * 32 + kc) * VS:(h * 32 + kc) * VS + 64],
                                KT[h * 64:(h + 1) * 64, kc * 128:(kc + 1) * 128]),
                                waits=w, inc="s_tr", by=16)
                    for kc in range(4 * tb, 4 * tb + 4):
                        for h in range(2):
                            vtr_toks[(h, kc)] = last_vtr
        proj_evac_done = ps_pr_free[(NTB - 1) % 2]
        proj_done_tok[p] = proj_last_mm
        if p + 1 < NPAIR:
            emit_wslice(p + 1)

        if pending[0] is not None:
            for c in pending[0]:
                c()
            pending[0] = None
        for (SRC, col) in ((QT, 0), (KT, 8)):
            for tb in range(NTB):
                sp = 0
                sqtok = em("vector", lambda v, SRC=SRC, tb=tb, sp=sp: v.tensor_tensor(
                    sqs[sp][:, :], SRC[:, tb * 512:(tb + 1) * 512],
                    SRC[:, tb * 512:(tb + 1) * 512], ALU.mult),
                    waits=[proj_evac_done, sq_free[sp]], inc="s_v")
                pp = tb % 2
                sq_free[1] = None
                qntok = em("tensor", lambda t, sp=sp, pp=pp: t.matmul(
                    ps_pr[pp][:, :], onesK[:, :], sqs[sp][:, :],
                    start=True, stop=True),
                    waits=[sqtok, t_oK, ps_pr_free[pp]], inc="s_t")
                sq_free[sp] = qntok
                ps_pr_free[pp] = em("vector", lambda v, col=col, tb=tb, pp=pp: v.reduce_max(
                    qparts[:, col + tb:col + tb + 1], ps_pr[pp][:, :], axis=AX.X),
                    waits=[qntok], inc="s_v")
        tq = em("vector", lambda v: v.reduce_max(
            normbuf[:, 0:1], qparts[:, 0:8], axis=AX.X),
            waits=[ps_pr_free[0], ps_pr_free[1], last_exp_tok], inc="s_v")
        tk2 = em("vector", lambda v: v.reduce_max(
            normbuf[:, 1:2], qparts[:, 8:16], axis=AX.X), waits=[tq], inc="s_v")
        biastok = em("vector", lambda v: v.tensor_scalar(
            normbuf[:, 2:3], normbuf[:, 0:1], normbuf[:, 1:2],
            -1.0 / 64.0, ALU.add, ALU.mult), waits=[tk2], inc="s_v")

        def make_chain(p, h, qb, evtoks):
            hb = h * 64

            def doit():
                rtok = evtoks[1]
                bctok = em("tensor", lambda t, h=h: t.matmul(
                    ps_pr[1][:, :], ones1x64[:, :], rinv_f[h][:, :], start=True, stop=True),
                    waits=[rtok, t_o64, ps_pr_free[1]], inc="s_t")
                rinv_free[h] = bctok
                cptok = em("vector", lambda v: v.tensor_copy(rden_f[:, :], ps_pr[1][:, :]),
                           waits=[bctok], inc="s_v")
                nm = em("vector", lambda v, p=p, hb=hb, qb=qb: v.tensor_tensor(
                    OT[p][hb:hb + 64, qb * 512:(qb + 1) * 512],
                    OT[p][hb:hb + 64, qb * 512:(qb + 1) * 512],
                    rden_f[hb:hb + 64, :], ALU.mult),
                    waits=[cptok], inc="s_v")
                ps_pr_free[1] = cptok
            return doit

        def emit_S(qb, kc):
            gb = kc % 2
            mm = None
            for h in range(2):
                hb = h * 64
                w = []
                if h == 0:
                    w = [ps_s_free[gb], biastok]
                mm = em("tensor", lambda t, hb=hb, qb=qb, kc=kc, h=h, gb=gb: t.matmul(
                    ps_s[gb][:, h * 512:(h + 1) * 512],
                    KT[hb:hb + 64, kc * 128:(kc + 1) * 128],
                    QT[hb:hb + 64, qb * 512:(qb + 1) * 512],
                    start=True, stop=True),
                    waits=w, inc="s_t" if h == 1 else None)
            return mm

        pv_last_h = [None, None]
        for qb in range(NTB):
            s_tok = emit_S(qb, 0)
            for kc in range(32):
                if kc == 8 and pending[0] is not None:
                    for c in pending[0]:
                        c()
                    pending[0] = None
                gb = kc % 2
                exptok = em("scalar", lambda sc, gb=gb: sc.activation(
                    PT[gb][:, :], ps_s[gb][:, :], AF.Exp,
                    bias=normbuf[:, 2:3], scale=1.0 / 32.0),
                    waits=[s_tok, pt_free[gb]], inc="s_sc")
                ps_s_free[gb] = exptok
                last_exp_tok = exptok
                if kc + 1 < 32:
                    s_tok = emit_S(qb, kc + 1)
                for h in range(2):
                    w = [exptok, vtr_toks[(h, kc)]]
                    if kc == 0:
                        w += [ps_o_free[h]]
                    pv_last_h[h] = em("tensor", lambda t, pb=pb, h=h, kc=kc, gb=gb: t.matmul(
                        ps_o[h][0:65, :],
                        Vp2[pb][:, (h * 32 + kc) * VS:(h * 32 + kc) * VS + 65],
                        PT[gb][:, h * 512:(h + 1) * 512],
                        start=(kc == 0), stop=(kc == 31)),
                        waits=w, inc="s_t")
                pt_free[gb] = pv_last_h[1]
            chains = []
            evs = []
            for h in range(2):
                hb = h * 64
                evo = em("vector", lambda v, p=p, hb=hb, qb=qb, h=h: v.tensor_copy(
                    OT[p][hb:hb + 64, qb * 512:(qb + 1) * 512], ps_o[h][0:64, :]),
                    waits=[pv_last_h[h]], inc="s_v")
                evr = em("vector", lambda v, h=h: v.tensor_copy(
                    rrow[h][:, :], ps_o[h][64:65, :]),
                    waits=[evo, rrow_free[h]], inc="s_v")
                ps_o_free[h] = evr
                evs.append((evo, evr))
            for h in range(2):
                def _rc(v, h=h):
                    with nc.allow_low_precision(reason="softmax 1/r broadcast operand is bf16"):
                        return v.reciprocal(rinv_f[h][:, :], rrow[h][:, :])
                rc = em("vector", _rc, waits=[evs[h][1], rinv_free[h]], inc="s_v")
                rrow_free[h] = rc
                chains.append(make_chain(p, h, qb, [evs[h][0], rc]))
            pending[0] = chains
        vp_free[pb] = pv_last_h[1]
        qt_free = pv_last_h[1]

    if pending[0] is not None:
        for c in pending[0]:
            c()
        pending[0] = None
    barrA = [b.snap("s_t"), b.snap("s_sc"), b.snap("s_v"), b.snap("s_tr"), b.snap("s_io")]

    scPair.close()

    NB = 4
    scOP = ExitStack()
    Wo_bf = sb(scOP, "Wo_bf", [128, 8192], BF16)
    xres = [sb(scOP, f"xres{i}", [128, E], F32) for i in range(NB)]
    ubuf = [sb(scOP, f"ubuf{i}", [128, E], F32) for i in range(NB)]
    usq_dump = sb(scOP, "usq_dump", [128, E], BF16)
    x1f = [sb(scOP, f"x1f{i}", [128, E], F32) for i in range(NB)]
    stats = sb(scOP, "stats", [128, 16 * NB], F32)
    ps_op = [psum(scOP, f"ps_op{i}", [128, 1024]) for i in range(3)]
    ps_t1 = [psum(scOP, f"ps_t1{i}", [128, 128]) for i in range(2)]

    wo_tok = em("gpsimd", lambda g: g.dma_start(
        out=Wo_bf[:, :].rearrange("p (dc c) -> p dc c", c=1024),
        in_=wo_h[:, :].rearrange("(dc p) c -> p dc c", p=128)),
        waits=barrA, inc="s_w", by=16)

    x1_tr_tok = [None] * NT
    trq = []
    ps_op_free = [None] * 3
    ubuf_free = [None] * NB
    x1f_free = [[], [], [], []]
    t1_free = [None, None]
    st_total = 0

    for ti in range(NT):
        tp = ti % NB
        tq_ = ti % 3
        so = 16 * tp
        xl = em("sync", lambda s, ti=ti, tp=tp: s.dma_start(
            out=xres[tp][:, :], in_=x_h[ti * 128:(ti + 1) * 128, :]),
            waits=barrA + [ubuf_free[tp]], inc="s_io", by=16)
        mmtok = None
        for cc2 in range(2):
            for dc in range(8):
                w = []
                if cc2 == 0 and dc == 0:
                    w = barrA + [wo_tok, ps_op_free[tq_]]
                mmtok = em("tensor", lambda t, ti=ti, tp=tq_, cc2=cc2, dc=dc: t.matmul(
                    ps_op[tp][:, cc2 * 512:(cc2 + 1) * 512],
                    OT[dc][:, ti * 128:(ti + 1) * 128],
                    Wo_bf[:, dc * 1024 + cc2 * 512:dc * 1024 + (cc2 + 1) * 512],
                    start=(dc == 0), stop=(dc == 7)),
                    waits=w, inc="s_t" if (dc == 7 and cc2 == 1) else None)
        ut1 = em("vector", lambda v, tp=tp: v.tensor_tensor(
            ubuf[tp][:, :], xres[tp][:, :], bo_bc[:, :], ALU.add),
            waits=[xl, t_bo] + x1f_free[tp], inc="s_v")
        ut2 = em("vector", lambda v, tp=tp, tq_=tq_: v.tensor_tensor(
            ubuf[tp][:, :], ubuf[tp][:, :], ps_op[tq_][:, :], ALU.add),
            waits=[ut1, mmtok], inc="s_v")
        ps_op_free[tq_] = ut2
        sstok = em("scalar", lambda sc, tp=tp, so=so: sc.activation(
            usq_dump[:, :], ubuf[tp][:, :], AF.Square, accum_out=stats[:, so:so + 1]),
            waits=[ut2], inc="s_sc")
        s1tok = em("vector", lambda v, tp=tp, so=so: v.reduce_sum(
            stats[:, so + 1:so + 2], ubuf[tp][:, :], axis=AX.X), waits=[ut2], inc="s_v")
        smtok = em("vector", lambda v, so=so: v.tensor_scalar(
            stats[:, so + 5:so + 6], stats[:, so + 1:so + 2], 1.0 / E, None, ALU.mult),
            waits=[s1tok], inc="s_v")
        sm2tok = em("vector", lambda v, so=so: v.tensor_scalar(
            stats[:, so + 6:so + 7], stats[:, so + 5:so + 6], stats[:, so + 5:so + 6],
            None, ALU.mult), waits=[smtok], inc="s_v")
        vartok = em("vector", lambda v, so=so: v.tensor_scalar(
            stats[:, so + 7:so + 8], stats[:, so:so + 1], 1.0 / E,
            stats[:, so + 6:so + 7], ALU.mult, ALU.subtract),
            waits=[sm2tok, sstok], inc="s_v")
        lntok = em("scalar", lambda sc, so=so: sc.activation(
            stats[:, so + 8:so + 9], stats[:, so + 7:so + 8], AF.Ln, bias=epsc[:, 0:1]),
            waits=[vartok, t_eps], inc="s_sc")
        rstok = em("scalar", lambda sc, so=so: sc.activation(
            stats[:, so + 9:so + 10], stats[:, so + 8:so + 9], AF.Exp, scale=-0.5),
            waits=[lntok], inc="s_sc")
        x1tok = em("vector", lambda v, tp=tp, so=so: v.tensor_scalar(
            x1f[tp][:, :], ubuf[tp][:, :], stats[:, so + 5:so + 6],
            stats[:, so + 9:so + 10], ALU.subtract, ALU.mult),
            waits=[rstok], inc="s_v")
        ubuf_free[tp] = x1tok
        x1st = em("sync", lambda s, ti=ti, tp=tp: s.dma_start(
            out=x1s_h[ti * 128:(ti + 1) * 128, :], in_=x1f[tp][:, :]),
            waits=[x1tok], inc="s_io", by=16)
        frees = [x1st]
        if mode == "ln1":
            frees.append(em("sync", lambda s, ti=ti, tp=tp: s.dma_start(
                out=out_h[ti * 128:(ti + 1) * 128, :], in_=x1f[tp][:, :]),
                waits=[x1tok], inc="s_st", by=16))
            st_total += 16
        def emit_x1tr(ti, tp, x1tok, frees):
            for cc in range(8):
                sl = cc % 2
                trt = em("tensor", lambda t, tp=tp, cc=cc, sl=sl: t.transpose(
                    ps_t1[sl][:, :], x1f[tp][:, cc * 128:(cc + 1) * 128], ident[:, :]),
                    waits=[x1tok, t1_free[sl]], inc="s_t")
                t1_free[sl] = em("vector", lambda v, ti=ti, cc=cc, sl=sl: v.tensor_copy(
                    xT[cc][:, ti * 128:(ti + 1) * 128], ps_t1[sl][:, :]),
                    waits=[trt], inc="s_v")
            x1_tr_tok[ti] = t1_free[(8 - 1) % 2]
            frees.append(b.snap("s_t"))
            x1f_free[tp] = frees
        trq.append((ti, tp, x1tok, frees))
        if len(trq) > 3:
            emit_x1tr(*trq.pop(0))
        if ti == NT - 1:
            while trq:
                emit_x1tr(*trq.pop(0))

    if mode == "ln1":
        em("vector", lambda v: v.memset(stats[:, 15:16], 0.0),
           waits=[("s_st", st_total)], inc="s_v")
        blk = ctx.enter_context(nc.Block())
        for eng in ENGINES:
            getattr(blk, eng)(lambda obj, e=eng: b.replay(e, obj))
        scOP.close()
        scOT.close()
        ctx.close()
        return nc

    barrOP = [b.snap("s_t"), b.snap("s_sc"), b.snap("s_v"), b.snap("s_io")]
    scOP.close()

    scR = ExitStack()
    rw_bf = sb(scR, "rw_bf", [16, E], BF16)
    rwT = sb(scR, "rwT", [128, 128], BF16)
    lg = sb(scR, "lg", [128, 256], F32)
    Ee = sb(scR, "Ee", [128, 256], F32)
    msk = sb(scR, "msk", [128, 256], F32)
    gee = sb(scR, "gee", [128, 256], F32)
    m1b = sb(scR, "m1b", [128, 32], F32)
    m2b = sb(scR, "m2b", [128, 32], F32)
    s12 = sb(scR, "s12", [128, 32], F32)
    ri32 = sb(scR, "ri32", [128, 32], F32)
    m20c = sb(scR, "m20c", [128, 1], F32)
    ps_r = [psum(scR, f"ps_r{i}", [128, 8]) for i in range(2)]

    def g3(ap):
        return ap.rearrange("p (t e) -> p t e", e=8)

    def b3(ap):
        return ap.rearrange("p (t o) -> p t o", o=1).broadcast_to([128, 32, 8])

    rw_ld = em("gpsimd", lambda g: g.dma_start(out=rw_bf[0:8, :], in_=rw_h[:, :]),
               waits=barrOP, inc="s_w", by=16)
    t_m20 = em("vector", lambda v: v.memset(m20c[:, :], -20.0), inc="s_v")
    rwt_tok = None
    for cc in range(8):
        rwt_tok = em("sync", lambda s, cc=cc: s.dma_start_transpose(
            rwT[:, cc * 16:(cc + 1) * 16], rw_bf[:, cc * 128:(cc + 1) * 128]),
            waits=[rw_ld], inc="s_tr", by=16)

    ps_r_free = [None, None]
    ev_last = None
    for ti in range(NT):
        rb = ti % 2
        mmtok = None
        for cc in range(8):
            w = []
            if cc == 0:
                w = [rwt_tok, x1_tr_tok[ti], ps_r_free[rb]] + (barrOP if ti == 0 else [])
            mmtok = em("tensor", lambda t, ti=ti, cc=cc, rb=rb: t.matmul(
                ps_r[rb][:, :], xT[cc][:, ti * 128:(ti + 1) * 128], rwT[:, cc * 16:cc * 16 + 8],
                start=(cc == 0), stop=(cc == 7)), waits=w, inc="s_t" if cc == 7 else None)
        ev_last = em("vector", lambda v, ti=ti, rb=rb: v.tensor_copy(
            lg[:, ti * 8:(ti + 1) * 8], ps_r[rb][:, :]), waits=[mmtok], inc="s_v")
        ps_r_free[rb] = ev_last
    ex = em("scalar", lambda sc: sc.activation(Ee[:, :], lg[:, :], AF.Exp, bias=m20c[:, 0:1]),
            waits=[ev_last, t_m20], inc="s_sc")
    c1 = em("vector", lambda v: v.reduce_max(m1b[:, :], g3(Ee[:, :]), axis=AX.X),
            waits=[ex], inc="s_v")
    c2 = em("vector", lambda v: v.tensor_tensor(g3(msk[:, :]), g3(Ee[:, :]), b3(m1b[:, :]),
                                                ALU.is_ge), waits=[c1], inc="s_v")
    c3 = em("vector", lambda v: v.scalar_tensor_tensor(msk[:, :], msk[:, :], -1e30,
                                                       Ee[:, :], ALU.mult, ALU.add),
            waits=[c2], inc="s_v")
    c4 = em("vector", lambda v: v.reduce_max(m2b[:, :], g3(msk[:, :]), axis=AX.X),
            waits=[c3], inc="s_v")
    c5 = em("vector", lambda v: v.tensor_tensor(g3(gee[:, :]), g3(Ee[:, :]), b3(m2b[:, :]),
                                                ALU.is_ge), waits=[c4], inc="s_v")
    c6 = em("vector", lambda v: v.tensor_tensor(s12[:, :], m1b[:, :], m2b[:, :], ALU.add),
            waits=[c4], inc="s_v")
    c7 = em("vector", lambda v: v.reciprocal(ri32[:, :], s12[:, :]), waits=[c6], inc="s_v")
    c8 = em("vector", lambda v: v.tensor_tensor(gee[:, :], gee[:, :], Ee[:, :], ALU.mult),
            waits=[c5], inc="s_v")
    em("vector", lambda v: v.tensor_tensor(g3(comb_sb[:, :]), g3(gee[:, :]), b3(ri32[:, :]),
                                           ALU.mult), waits=[c7, c8], inc="s_v")
    comb_done = b.snap("s_v")

    barrB = [b.snap("s_t"), b.snap("s_sc"), b.snap("s_v"), b.snap("s_tr"), b.snap("s_io")]
    scR.close()
    scOT.close()

    scF = ExitStack()
    acc = sb(scF, "acc", [128, 8 * E], F32)
    w1b = [sb(scF, f"w1b{i}", [128, 8192], BF16) for i in range(2)]
    w2b = [sb(scF, f"w2b{i}", [128, 8192], BF16) for i in range(2)]
    hT = [sb(scF, f"hT{i}", [128, 512], BF16) for i in range(8)]
    x1res = [sb(scF, f"x1res{i}", [128, E], F32) for i in range(2)]
    u2buf = [sb(scF, f"u2buf{i}", [128, E], F32) for i in range(2)]
    o2buf = [sb(scF, f"o2buf{i}", [128, E], F32) for i in range(2)]
    usq2 = sb(scF, "usq2", [128, E], BF16)
    stats2 = sb(scF, "stats2", [128, 32], F32)
    ps_h = [psum(scF, f"ps_h{i}", [128, 512]) for i in range(2)]
    ps_y = [psum(scF, f"ps_y{i}", [128, 512]) for i in range(2)]

    NSLICE = NQ * NEXP * NHG
    wready = [None] * NSLICE
    wfree = [None] * NSLICE

    def emit_wdma(s):
        q, rem = divmod(s, NEXP * NHG)
        e, hg = divmod(rem, NHG)
        buf = s % 2
        war = wfree[s - 2] if s >= 2 else None
        w = [war] + (barrB if s < 2 else [])
        em("gpsimd", lambda g, e=e, hg=hg, buf=buf: g.dma_start(
            out=w1b[buf][:, :].rearrange("p (cc h) -> p cc h", h=1024),
            in_=w1_h[e, :, hg * 1024:(hg + 1) * 1024].rearrange("(cc p) h -> p cc h", p=128)),
            waits=w, inc="s_w", by=16)
        wready[s] = em("gpsimd", lambda g, e=e, hg=hg, buf=buf: g.dma_start(
            out=w2b[buf][:, :].rearrange("p (hc c) -> p hc c", c=1024),
            in_=w2_h[e, hg * 1024:(hg + 1) * 1024, :].rearrange("(hc p) c -> p hc c", p=128)),
            waits=w, inc="s_w", by=16)

    emit_wdma(0)
    emit_wdma(1)

    ps_h_free = [None, None]
    ps_y_free = [None, None]
    st_total = 0
    x1res_free = [None, None]
    u2_free = [None, None]
    o2_free = [None, None]

    for q in range(NQ):
        for e in range(NEXP):
            for hg in range(NHG):
                s = (q * NEXP + e) * NHG + hg
                buf = s % 2
                wlast = None
                for tb in range(2):
                    tok0 = q * QTOK + tb * 512
                    gelu_toks = []
                    for hc in range(8):
                        hb2 = hc % 2
                        mmtok = None
                        for cc in range(8):
                            w = []
                            if cc == 0:
                                w = [wready[s], ps_h_free[hb2]]
                                if e == 0 and hg == 0 and tb == 0 and hc == 0:
                                    w = w + [x1_tr_tok[q * 8 + 7]]
                                if s < 2 and tb == 0 and hc == 0:
                                    w = w + barrB
                            mmtok = em("tensor", lambda t, buf=buf, cc=cc, hc=hc, tok0=tok0, hb2=hb2: t.matmul(
                                ps_h[hb2][:, :],
                                w1b[buf][:, cc * 1024 + hc * 128:cc * 1024 + (hc + 1) * 128],
                                xT[cc][:, tok0:tok0 + 512],
                                start=(cc == 0), stop=(cc == 7)),
                                waits=w, inc="s_t" if cc == 7 else None)
                        gt = em("scalar", lambda sc, hc=hc, hb2=hb2: sc.activation(
                            hT[hc][:, :], ps_h[hb2][:, :], AF.Gelu), waits=[mmtok], inc="s_sc")
                        ps_h_free[hb2] = gt
                        gelu_toks.append(gt)
                    for tt2 in range(4):
                        ti_q = tb * 4 + tt2
                        tg = q * 8 + ti_q
                        for cc2 in range(2):
                            yb = (tt2 * 2 + cc2) % 2
                            mmtok = None
                            for hc in range(8):
                                w = [gelu_toks[hc]]
                                if hc == 0:
                                    w.append(ps_y_free[yb])
                                mmtok = em("tensor", lambda t, buf=buf, hc=hc, tt2=tt2, cc2=cc2, yb=yb: t.matmul(
                                    ps_y[yb][:, :],
                                    hT[hc][:, tt2 * 128:(tt2 + 1) * 128],
                                    w2b[buf][:, hc * 1024 + cc2 * 512:hc * 1024 + (cc2 + 1) * 512],
                                    start=(hc == 0), stop=(hc == 7)),
                                    waits=w, inc="s_t" if hc == 7 else None)
                            wlast = mmtok
                            if e == 0 and hg == 0:
                                ev = em("vector", lambda v, ti_q=ti_q, cc2=cc2, yb=yb, tg=tg, e=e: v.tensor_scalar(
                                    acc[:, ti_q * 1024 + cc2 * 512:ti_q * 1024 + (cc2 + 1) * 512],
                                    ps_y[yb][:, :], comb_sb[:, tg * 8 + e:tg * 8 + e + 1],
                                    None, ALU.mult),
                                    waits=[mmtok, comb_done if s == 0 else None], inc="s_v")
                            else:
                                ev = em("vector", lambda v, ti_q=ti_q, cc2=cc2, yb=yb, tg=tg, e=e: v.scalar_tensor_tensor(
                                    acc[:, ti_q * 1024 + cc2 * 512:ti_q * 1024 + (cc2 + 1) * 512],
                                    ps_y[yb][:, :], comb_sb[:, tg * 8 + e:tg * 8 + e + 1],
                                    acc[:, ti_q * 1024 + cc2 * 512:ti_q * 1024 + (cc2 + 1) * 512],
                                    ALU.mult, ALU.add), waits=[mmtok], inc="s_v")
                            ps_y_free[yb] = ev
                wfree[s] = wlast
                if s + 2 < NSLICE:
                    emit_wdma(s + 2)
        for ti_q in range(8):
            ti = q * 8 + ti_q
            tp = ti_q % 2
            so = 16 * tp
            xl = em("sync", lambda sy, ti=ti, tp=tp: sy.dma_start(
                out=x1res[tp][:, :], in_=x1s_h[ti * 128:(ti + 1) * 128, :]),
                waits=x1f_free[0] + x1f_free[1] + x1f_free[2] + x1f_free[3] + [x1res_free[tp]],
                inc="s_io", by=16)
            ut = em("vector", lambda v, ti_q=ti_q, tp=tp: v.tensor_tensor(
                u2buf[tp][:, :], acc[:, ti_q * 1024:(ti_q + 1) * 1024], x1res[tp][:, :], ALU.add),
                waits=[xl, ps_y_free[0], ps_y_free[1], u2_free[tp]], inc="s_v")
            x1res_free[tp] = ut
            sstok = em("scalar", lambda sc, tp=tp, so=so: sc.activation(
                usq2[:, :], u2buf[tp][:, :], AF.Square, accum_out=stats2[:, so:so + 1]),
                waits=[ut], inc="s_sc")
            s1tok = em("vector", lambda v, tp=tp, so=so: v.reduce_sum(
                stats2[:, so + 1:so + 2], u2buf[tp][:, :], axis=AX.X), waits=[ut], inc="s_v")
            smtok = em("vector", lambda v, so=so: v.tensor_scalar(
                stats2[:, so + 5:so + 6], stats2[:, so + 1:so + 2], 1.0 / E, None,
                ALU.mult), waits=[s1tok], inc="s_v")
            sm2tok = em("vector", lambda v, so=so: v.tensor_scalar(
                stats2[:, so + 6:so + 7], stats2[:, so + 5:so + 6],
                stats2[:, so + 5:so + 6], None, ALU.mult), waits=[smtok], inc="s_v")
            vartok = em("vector", lambda v, so=so: v.tensor_scalar(
                stats2[:, so + 7:so + 8], stats2[:, so:so + 1],
                1.0 / E, stats2[:, so + 6:so + 7], ALU.mult, ALU.subtract),
                waits=[sm2tok, sstok], inc="s_v")
            lntok = em("scalar", lambda sc, so=so: sc.activation(
                stats2[:, so + 8:so + 9], stats2[:, so + 7:so + 8], AF.Ln, bias=epsc[:, 0:1]),
                waits=[vartok, t_eps], inc="s_sc")
            rstok = em("scalar", lambda sc, so=so: sc.activation(
                stats2[:, so + 9:so + 10], stats2[:, so + 8:so + 9], AF.Exp, scale=-0.5),
                waits=[lntok], inc="s_sc")
            otok = em("vector", lambda v, tp=tp, so=so: v.tensor_scalar(
                o2buf[tp][:, :], u2buf[tp][:, :], stats2[:, so + 5:so + 6],
                stats2[:, so + 9:so + 10], ALU.subtract, ALU.mult),
                waits=[rstok, o2_free[tp]], inc="s_v")
            u2_free[tp] = otok
            sttok = em("sync", lambda sy, ti=ti, tp=tp: sy.dma_start(
                out=out_h[ti * 128:(ti + 1) * 128, :], in_=o2buf[tp][:, :]),
                waits=[otok], inc="s_st", by=16)
            o2_free[tp] = sttok
            st_total += 16

    em("vector", lambda v: v.memset(stats2[:, 15:16], 0.0),
       waits=[("s_st", st_total)], inc="s_v")

    blk = ctx.enter_context(nc.Block())
    for eng in ENGINES:
        getattr(blk, eng)(lambda obj, e=eng: b.replay(e, obj))
    scF.close()
    ctx.close()
    return nc


_prog_cache = {}


def _get_prog(mode):
    if mode not in _prog_cache:
        _prog_cache[mode] = build_program(mode)
    return _prog_cache[mode]


def kernel(L0_attn_x, L0_attn_Wk, L0_attn_Wq, L0_attn_Wo, L0_attn_bo,
           L0_ffn_router_w, L0_ffn_w1, L0_ffn_w2, mode="full"):
    x = np.ascontiguousarray(np.asarray(L0_attn_x, dtype=np.float32))
    wk = np.ascontiguousarray(np.asarray(L0_attn_Wk, dtype=np.float32))
    wq = np.ascontiguousarray(np.asarray(L0_attn_Wq, dtype=np.float32))
    wo = np.ascontiguousarray(np.asarray(L0_attn_Wo, dtype=np.float32))
    bo = np.ascontiguousarray(np.broadcast_to(
        np.asarray(L0_attn_bo, dtype=np.float32).reshape(1, E), (128, E)))
    rw = np.ascontiguousarray(np.asarray(L0_ffn_router_w, dtype=np.float32))
    w1 = np.ascontiguousarray(np.asarray(L0_ffn_w1, dtype=np.float32))
    w2 = np.ascontiguousarray(np.asarray(L0_ffn_w2, dtype=np.float32))

    nc = _get_prog(mode)
    ident_np = np.eye(128, dtype=np.float32)
    ncores = int(os.environ.get("KN_CORES", "8"))
    in_maps = []
    for c in range(ncores):
        in_maps.append({"x": x[c], "wq": wq, "wk": wk, "wo": wo, "bo": bo,
                        "rw": rw, "w1": w1, "w2": w2, "ident_in": ident_np})
    res = run_bass_kernel_spmd(nc, in_maps, core_ids=list(range(ncores)))
    return np.stack([r["out"] for r in res.results], axis=0)
